# version 02
# baseline (speedup 1.0000x reference)
import numpy as np
import ml_dtypes
from contextlib import ExitStack
import concourse.bass as bass
import concourse.mybir as mybir
from concourse.bass_utils import run_bass_kernel_spmd

F32 = mybir.dt.float32
BF16 = mybir.dt.bfloat16
I32 = mybir.dt.int32
AF = mybir.ActivationFunctionType
ALU = mybir.AluOpType
AX = mybir.AxisListType

D = 4096
TOK = 1024
NTC = 8
NDJ = 32
EPS = 1e-5
INVD = 1.0 / D
C_A = float(1.0 / np.sqrt(128.0))
NEG = -3.0e38
STAGE = 4
DEBUG_STOP = None

W_QA, W_KA, W_VA, W_OA, W_UP, W_QB, W_KB, W_VB = range(8)
FLAG_KVA, FLAG_KVB = 8, 9
MAGICS = np.array([1094795585 + 7919 * i for i in range(16)], dtype=np.int32)


def _blade_tables():
    g = [1.0, 1.0, 1.0]
    n = 8
    masks = sorted(range(n), key=lambda m: (bin(m).count('1'), m))
    inv = {m: i for i, m in enumerate(masks)}
    C = np.zeros((n, n, n), dtype=np.float32)
    for i, a in enumerate(masks):
        for j, b2 in enumerate(masks):
            sw, at = (0, a >> 1)
            while at:
                sw += bin(at & b2).count('1')
                at >>= 1
            s = -1.0 if sw & 1 else 1.0
            C[i, j, inv[a ^ b2]] = s
    grades = np.array([bin(m).count('1') for m in masks])
    rev = np.where(grades * (grades - 1) // 2 % 2 == 1, -1.0, 1.0).astype(np.float32)
    return C, rev


def _d0g_consts():
    C, rev = _blade_tables()
    scale = 1.0 / np.sqrt(256.0)
    D0 = np.kron(np.eye(16, dtype=np.float32), np.diag(rev * scale))
    Gs = [np.kron(np.eye(16, dtype=np.float32), rev[:, None] * C[:, :, c]) for c in (4, 5, 6)]
    return np.stack([D0] + Gs).astype(np.float32)


class _DrainInstr:
    def __init__(self, eng, instr):
        self._eng, self._instr = eng, instr

    def then_inc(self, sem, n):
        return self._eng.drain().then_inc(sem, n)


_DRAIN_OPS = {"tensor_scalar", "tensor_tensor", "tensor_copy", "tensor_reduce",
              "reduce_sum", "reduce_max", "reciprocal", "memset",
              "scalar_tensor_tensor", "activation", "bn_stats", "bn_aggr",
              "engine_nop", "tensor_scalar_mul", "tensor_add", "tensor_sub",
              "tensor_mul", "tensor_max", "matmul", "transpose"}


class _DrainProxy:
    def __init__(self, eng):
        self._e = eng

    def __getattr__(self, nm):
        a = getattr(self._e, nm)
        if nm in _DRAIN_OPS:
            def w(*args, **kw):
                return _DrainInstr(self._e, a(*args, **kw))
            return w
        return a


class KB:

    def __init__(self, nc, stage):
        self.nc = nc
        self.stage = stage
        self.steps = {k: [] for k in "tvags"}
        self.cnt = {}
        self.ctx = ExitStack()

    def st(self, engk, fn):
        self.steps[engk].append(fn)

    def at(self, semk):
        return self.cnt.get(semk, 0)

    def bump(self, semk, n=1):
        self.cnt[semk] = self.cnt.get(semk, 0) + n
        return self.cnt[semk]

    def wait(self, engk, semk, val=None):
        v = self.at(semk) if val is None else val
        if v <= 0:
            return
        s = self.sem[semk]
        self.st(engk, lambda e, r, s=s, v=v: e.wait_ge(s, v))

    def inc_of(self, semk):
        return self.sem[semk]

    def waitall(self, engk, semks):
        for k in semks:
            self.wait(engk, k)

    def dma(self, engk, dst, src, semk):
        s = self.sem[semk]
        self.bump(semk, 16)
        self.st(engk, lambda e, r, d=dst, so=src, s=s: e.dma_start(out=d, in_=so).then_inc(s, 16))

    def dma_par(self, engk, flat_dst, rows, cols, pbase, fixed, src, semk, store=True):
        s = self.sem[semk]
        self.bump(semk, 16)
        regn = "gp" if engk == "g" else "prow"

        def f(e, r, flat_dst=flat_dst, rows=rows, cols=cols, pbase=pbase,
              fixed=fixed, src=src, s=s, regn=regn):
            e.reg_load(r[regn], self.spar[0:1, 0:1])
            e.reg_alu(r[regn], r[regn], pbase, ALU.mult)
            e.reg_alu(r[regn], r[regn], fixed, ALU.add)
            view = flat_dst[bass.ds(e.snap(r[regn]), rows), cols] if store else None
            if store:
                e.dma_start(out=view, in_=src).then_inc(s, 16)
            else:
                e.dma_start(out=src, in_=flat_dst[bass.ds(e.snap(r[regn]), rows), cols]).then_inc(s, 16)
        self.st(engk, f)

    def build(self):
        nc = self.nc
        ctx = self.ctx
        self.xc = nc.dram_tensor("xc", [TOK, D], F32, kind="ExternalInput").ap()
        self.wT = [nc.dram_tensor(f"w{i}", [D, 2048], F32, kind="ExternalInput").ap() for i in range(8)]
        self.bcols = nc.dram_tensor("bcols", [128, 128], F32, kind="ExternalInput").ap()
        self.bbcast = nc.dram_tensor("bbcast", [3, 128, D], F32, kind="ExternalInput").ap()
        self.ebI = nc.dram_tensor("ebI", [128, 4], F32, kind="ExternalInput").ap()
        self.bwI = nc.dram_tensor("bwI", [128, 48], F32, kind="ExternalInput").ap()
        self.gwTI = nc.dram_tensor("gwTI", [D, 4], F32, kind="ExternalInput").ap()
        self.acatI = nc.dram_tensor("acatI", [D, 32], F32, kind="ExternalInput").ap()
        self.bcatI = nc.dram_tensor("bcatI", [32, D], F32, kind="ExternalInput").ap()
        self.p48I = nc.dram_tensor("p48I", [4, 32], F32, kind="ExternalInput").ap()
        self.idI = nc.dram_tensor("idI", [128, 128], F32, kind="ExternalInput").ap()
        self.cmI = nc.dram_tensor("cmI", [128, 256], F32, kind="ExternalInput").ap()
        self.d0gI = nc.dram_tensor("d0gI", [4, 128, 128], F32, kind="ExternalInput").ap()
        self.parI = nc.dram_tensor("parI", [1, 1], I32, kind="ExternalInput").ap()
        self.magI = nc.dram_tensor("magI", [1, 16], I32, kind="ExternalInput").ap()
        self.yc = nc.dram_tensor("yc", [TOK, D], F32, kind="ExternalOutput").ap()
        self.xres = [nc.dram_tensor(f"xres{i}", [TOK, D], F32).ap() for i in range(2)]
        self.qx = nc.dram_tensor("qx", [D, TOK], BF16).ap()
        self.kl = nc.dram_tensor("kl", [D, TOK], BF16).ap()
        self.vl = nc.dram_tensor("vl", [TOK, D], BF16).ap()
        self.wbf = nc.dram_tensor("wbf", [D, 8 * D], BF16, addr_space="Shared").ap()
        self.kx = [nc.dram_tensor(f"kx{i}", [2 * D, TOK], BF16, addr_space="Shared").ap() for i in range(2)]
        self.vx = [nc.dram_tensor(f"vx{i}", [2 * TOK, D], BF16, addr_space="Shared").ap() for i in range(2)]
        self.shf = nc.dram_tensor("shf", [2, 16], I32, addr_space="Shared").ap()

        sb = lambda nm, shp, dt: ctx.enter_context(nc.sbuf_tensor(nm, shp, dt))
        self.bigA = sb("bigA", [128, NDJ * TOK], BF16)
        self.bigB = sb("bigB", [128, NTC * D], BF16)
        self.arena = sb("arena", [128, 32768], BF16)
        self.xt = self.arena[:, 0:16384].bitcast(F32)
        self.jnk = self.arena[:, 16384:20480]
        self.xnb = self.arena[:, 20480:24576]
        self.wsl = self.arena[:, 0:32768]
        self.ktile = self.arena[:, 0:8192]
        self.vtile = self.arena[:, 8192:16384]
        self.qtile = self.arena[:, 16384:20480]
        self.sqeff = self.arena[:, 20480:24576]
        self.spp = self.arena[:, 24576:28672]
        self.spt = self.arena[:, 28672:32768]
        self.xadd_in = self.arena[:, 0:16384].bitcast(F32)
        self.xadd_out = self.arena[:, 16384:32768].bitcast(F32)
        self.sohT = self.bigB[0:4, 0:1024]
        self.soh32 = self.bigB[0:32, 1024:2048]
        self.szT = self.bigB[0:32, 2048:3072]
        self.szm = self.bigB[0:32, 3072:4096]
        self.sbcat = self.bigB[0:32, 4096:8192]
        self.sd0g = self.bigB[:, 8192:9216].bitcast(F32)
        self.smhf = self.bigB[:, 9216:9472].bitcast(F32)
        self.sacat = self.bigB[:, 10240:11264]
        self.stats = sb("stats", [128, 32], F32)
        self.rstd = sb("rstd", [128, 8], F32)
        self.sbias = sb("sbias", [128, 128], F32)
        self.sbb = sb("sbb", [128, 2 * 512], BF16)
        self.seb = sb("seb", [128, 4], F32)
        self.sbw = sb("sbw", [128, 48], F32)
        self.sgw = sb("sgw", [128, NDJ * 4], BF16)
        self.sp48 = sb("sp48", [4, 32], BF16)
        self.sid = sb("sid", [128, 128], BF16)
        self.spar = sb("spar", [1, 1], I32)
        self.smag = sb("smag", [1, 16], I32)
        self.spoll = sb("spoll", [1, 1], I32)
        self.smh = sb("smh", [128, 16 * 128], BF16)
        self.scm = self.smh[:, 0:512].bitcast(F32)
        self.stgar = sb("stgar", [128, 4096], BF16)
        self.stg = self.stgar[:, 0:2048]
        self.stgf = self.stgar[:, 0:2048].bitcast(F32)
        self.sglog = sb("sglog", [128, NTC * 4], F32)
        self.sgmax = sb("sgmax", [128, NTC], F32)
        self.soh = sb("soh", [128, NTC * 4], BF16)
        self.sneg = sb("sneg", [128, 16], F32)
        self.sjnk = self.sgmax
        self.srl = self.sneg[:, 12:16]
        self.xsl = self.stgar[:, 2048:4096].bitcast(F32)
        self.PS = [ctx.enter_context(nc.psum_tensor("psA", [128, 2048], F32)),
                   ctx.enter_context(nc.psum_tensor("psB", [128, 2048], F32))]

        self.sem = {}
        for nm in ["lw0", "lw1", "lx0", "lx1", "ldp", "ldc", "ldb", "lk0", "lk1",
                   "lv0", "lv1", "cast", "cw0", "cw1", "cw2", "sk0", "sk1",
                   "sx0", "sx1", "sex", "poll", "pe", "dv", "ac", "gp", "dbg"]:
            self.sem[nm] = ctx.enter_context(nc.semaphore("s_" + nm))

        self.block = ctx.enter_context(nc.Block())

        self.emit_all()

        block = self.block
        steps = self.steps

        @block.tensor
        def _(t):
            r = {}
            for f in steps["t"]:
                f(t, r)

        @block.vector
        def _(v):
            r = {}
            for f in steps["v"]:
                f(v, r)

        @block.scalar
        def _(a):
            r = {}
            for f in steps["a"]:
                f(a, r)

        @block.gpsimd
        def _(g):
            r = {"gp": None}
            g_regs = g.register("gp")
            r["gp"] = g_regs.__enter__()
            for f in steps["g"]:
                f(g, r)
            g_regs.__exit__(None, None, None)

        @block.sync
        def _(s):
            r = {}
            hs = [s.register(n) for n in ("prow", "fval", "pcnt", "cond", "mg")]
            for h, n in zip(hs, ("prow", "fval", "pcnt", "cond", "mg")):
                r[n] = h.__enter__()
            s.reg_mov(r["pcnt"], 0)
            for f in steps["s"]:
                f(s, r)
            for h in hs:
                h.__exit__(None, None, None)

        self.ctx.close()
        return self.nc

    def emit_all(self):
        self.w_polled = [False] * 8
        self.w_cast_cnt = [0] * 8
        self.gp_mark = {}
        self.prelude()
        src = self.xc
        dsts = [self.xres[0], self.xres[1], self.xres[0], self.yc]
        if self.stage >= 1:
            d = dsts[0] if self.stage > 1 else self.yc
            self.sublayer_attn_a(src, d)
            src = d
        if self.stage >= 2:
            d = dsts[1] if self.stage > 2 else self.yc
            self.sublayer_ffn(src, d)
            src = d
        if self.stage >= 3:
            d = dsts[2] if self.stage > 3 else self.yc
            self.sublayer_attn_b(src, d)
            src = d
        if self.stage >= 4:
            self.sublayer_ffn(src, self.yc)
        self.waitall("s", ["sx0", "sx1", "sk0", "sk1", "sex"])

    def prelude(self):
        self.dma("s", self.spar[:], self.parI[:, :], "ldp")
        self.dma("s", self.smag[:], self.magI[:, :], "ldp")
        for dst, src in [(self.sbias[:], self.bcols[:, :]), (self.seb[:], self.ebI[:, :]),
                         (self.sbw[:], self.bwI[:, :]), (self.scm[:], self.cmI[:, :])]:
            self.dma("s", dst, src, "ldc")
        self.dma("g", self.sid[:], self.idI[:, :], "cast")
        self.dma("g", self.sgw[:, :].rearrange("p (a c) -> p a c", a=NDJ),
                 self.gwTI.rearrange("(a p) c -> p a c", p=128), "cast")
        self.dma("g", self.sp48[:, :], self.p48I[:, :], "cast")
        self.cast_pre = self.at("cast")
        self.bump("dv")
        sdvw = self.sem["dv"]
        self.st("v", lambda e, r, s=sdvw: e.memset(self.sjnk[:, :], 1.0).then_inc(s, 1))
        self.wait("a", "dv")
        self.bump("ac")
        sacw = self.sem["ac"]
        self.st("a", lambda e, r, s=sacw: e.activation(self.sjnk[:, 2:3], self.sjnk[:, 1:2], AF.Exp).then_inc(s, 1))
        self.wait("a", "ac")
        self.bump("ac")
        self.st("a", lambda e, r, s=sacw: e.activation(self.sjnk[:, 3:4], self.sjnk[:, 1:2], AF.Ln).then_inc(s, 1))
        self.bump("dv")
        sdv = self.sem["dv"]
        PS = self.PS
        self.st("v", lambda e, r: e.memset(PS[0][:, :], 0.0))
        self.st("v", lambda e, r, s=sdv, v=self.at("dv"): e.memset(PS[1][:, :], 0.0).then_inc(s, 1))
        self.psum_scrub = self.at("dv")
        self.waitall("g", ["ldp"])
        FLAG_GROUP = {0: 0, 1: 0, 2: 0, 3: 1, 4: 1, 5: 2, 6: 2, 7: 2}
        self.flag_group = FLAG_GROUP
        self.group_gp = {}
        GS = {0: "cw0", 1: "cw0", 2: "cw0", 3: "cw1", 4: "cw1", 5: "cw2", 6: "cw2", 7: "cw2"}
        self.gsem = GS
        for w in range(8):
            self.bump(GS[w], 16)
            scast = self.sem[GS[w]]

            def cast_w(e, r, w=w, s=scast):
                e.reg_load(r["gp"], self.spar[0:1, 0:1])
                e.reg_alu(r["gp"], r["gp"], 2048, ALU.mult)
                e.reg_alu(r["gp"], r["gp"], w * D, ALU.add)
                offw = e.snap(r["gp"])
                e.dma_start(out=self.wbf[:, bass.ds(offw, 2048)],
                            in_=self.wT[w][:, :]).then_inc(s, 16)
            self.st("g", cast_w)
            self.w_cast_cnt[w] = self.at(GS[w])
            if w in (2, 4, 7):
                self.wait("g", GS[w])
                self.flag_write({2: 10, 4: 11, 7: 12}[w])

    def kv_exchange(self, attn_idx, flag_idx):
        self.waitall("s", ["sk0", "sk1"])
        sstk = self.sem["sex"]
        kxf, vxf = self.kx[attn_idx], self.vx[attn_idx]
        self.bump("sex", 16)

        def cpk(e, r, kxf=kxf, s=sstk):
            e.reg_load(r["prow"], self.spar[0:1, 0:1])
            e.reg_alu(r["prow"], r["prow"], D, ALU.mult)
            e.dma_start(out=kxf[bass.ds(e.snap(r["prow"]), D), :], in_=self.kl[:, :]).then_inc(s, 16)
        self.st("s", cpk)
        self.bump("sex", 16)

        def cpv(e, r, vxf=vxf, s=sstk):
            e.reg_load(r["prow"], self.spar[0:1, 0:1])
            e.reg_alu(r["prow"], r["prow"], TOK, ALU.mult)
            e.dma_start(out=vxf[bass.ds(e.snap(r["prow"]), TOK), :], in_=self.vl[:, :]).then_inc(s, 16)
        self.st("s", cpv)
        self.flag_write(flag_idx, extra_wait=("sex", self.at("sex")))
        return self.poll(flag_idx)

    def flag_write(self, idx, extra_wait=None):
        if extra_wait is not None:
            self.wait("g", extra_wait[0], extra_wait[1])
        self.bump("cast", 16)
        scast = self.sem["cast"]

        def f(e, r, idx=idx, s=scast):
            e.reg_load(r["gp"], self.spar[0:1, 0:1])
            e.dma_start(out=self.shf.rearrange("a b -> (a b)")[bass.ds(e.snap(r["gp"]), 1)]
                        if False else
                        self.shf[0:2, idx:idx + 1].rearrange("a b -> (a b) b")[bass.ds(e.snap(r["gp"]), 1), :]
                        if False else
                        self.shf[:, idx:idx + 1][bass.ds(e.snap(r["gp"]), 1), :],
                        in_=self.smag[0:1, idx:idx + 1]).then_inc(s, 16)
        self.st("g", f)

    def poll(self, idx):
        spoll = self.sem["poll"]

        def f(e, r, idx=idx, s=spoll):
            e.reg_load(r["prow"], self.spar[0:1, 0:1])
            e.reg_alu(r["prow"], r["prow"], 1, ALU.bitwise_xor)
            e.reg_load(r["mg"], self.smag[0:1, idx:idx + 1])
            e.reg_mov(r["cond"], 1)
            with e.While(r["cond"]):
                e.dma_start(out=self.spoll[:],
                            in_=self.shf[:, idx:idx + 1][bass.ds(e.snap(r["prow"]), 1), :]).then_inc(s, 16)
                e.reg_alu(r["pcnt"], r["pcnt"], 16, ALU.add)
                e.wait_ge(s, e.snap(r["pcnt"]))
                e.reg_load(r["fval"], self.spoll[0:1, 0:1])
                e.reg_alu(r["cond"], r["fval"], e.snap(r["mg"]), ALU.not_equal)
        self.st("s", f)
        self.bump("gp")
        sgp = self.sem["gp"]
        self.st("s", lambda e, r, s=sgp: e.sem_inc(s, 1))
        return self.at("gp")

    def need_w(self, w):
        grp = self.flag_group[w]
        if grp not in self.group_gp:
            self.wait("s", ["cw0", "cw1", "cw2"][grp])
            self.group_gp[grp] = self.poll({0: 10, 1: 11, 2: 12}[grp])
        return self.group_gp[grp]

    def emit_ln(self, src):
        xn_done = {}
        ev_done = {}
        sdv, sac, spe = self.sem["dv"], self.sem["ac"], self.sem["pe"]
        for tc in range(NTC):
            b = tc % 2
            if tc >= 2:
                self.wait("s", "dv", xn_done[tc - 2])
            if tc == 0:
                self.waitall("s", ["sx0", "sx1"])
            self.dma("s", self.xt[:, b * D:(b + 1) * D], src[tc * 128:(tc + 1) * 128, :],
                     "lx0" if b == 0 else "lx1")
            self.wait("a", "lx0" if b == 0 else "lx1")
            self.wait("a", "pe")
            self.bump("ac")
            self.st("a", lambda e, r, b=b, c=tc, s=sac:
                    e.activation(self.xnb[:, :], self.xt[:, b * D:(b + 1) * D], AF.Copy,
                                 accum_out=self.stats[:, c * 4:c * 4 + 1]).then_inc(s, 1))
            self.bump("ac")
            self.st("a", lambda e, r, b=b, c=tc, s=sac:
                    e.activation(self.jnk[:, :], self.xt[:, b * D:(b + 1) * D], AF.Square,
                                 accum_out=self.stats[:, c * 4 + 1:c * 4 + 2]).then_inc(s, 1))
            self.wait("v", "ac")
            self.bump("dv")
            self.st("v", lambda e, r, c=tc, s=sdv:
                    e.tensor_scalar(self.stats[:, c * 4 + 2:c * 4 + 3], self.stats[:, c * 4:c * 4 + 1],
                                    -INVD, None, op0=ALU.mult).then_inc(s, 1))
            self.wait("v", "dv")
            self.bump("dv")
            self.st("v", lambda e, r, c=tc, s=sdv:
                    e.tensor_tensor(self.stats[:, c * 4 + 3:c * 4 + 4], self.stats[:, c * 4 + 2:c * 4 + 3],
                                    self.stats[:, c * 4 + 2:c * 4 + 3], op=ALU.mult).then_inc(s, 1))
            self.bump("dv")
            self.st("v", lambda e, r, c=tc, s=sdv:
                    e.tensor_scalar(self.rstd[:, c:c + 1], self.stats[:, c * 4 + 1:c * 4 + 2],
                                    INVD, EPS, op0=ALU.mult, op1=ALU.add).then_inc(s, 1))
            self.wait("v", "dv")
            self.bump("dv")
            self.st("v", lambda e, r, c=tc, s=sdv:
                    e.tensor_tensor(self.rstd[:, c:c + 1], self.rstd[:, c:c + 1],
                                    self.stats[:, c * 4 + 3:c * 4 + 4], op=ALU.subtract).then_inc(s, 1))
            self.wait("a", "dv")
            self.bump("ac")
            self.st("a", lambda e, r, c=tc, s=sac:
                    e.activation(self.rstd[:, c:c + 1], self.rstd[:, c:c + 1], AF.Ln).then_inc(s, 1))
            self.wait("v", "ac")
            self.bump("dv")
            self.st("v", lambda e, r, c=tc, s=sdv:
                    e.tensor_scalar(self.rstd[:, c:c + 1], self.rstd[:, c:c + 1], -0.5, None,
                                    op0=ALU.mult).then_inc(s, 1))
            self.wait("a", "dv")
            self.bump("ac")
            self.st("a", lambda e, r, c=tc, s=sac:
                    e.activation(self.rstd[:, c:c + 1], self.rstd[:, c:c + 1], AF.Exp).then_inc(s, 1))
            self.wait("v", "ac")
            self.wait("v", "pe")
            self.bump("dv")
            self.st("v", lambda e, r, b=b, c=tc, s=sdv:
                    e.tensor_scalar(self.xnb[:, :], self.xt[:, b * D:(b + 1) * D],
                                    self.stats[:, c * 4 + 2:c * 4 + 3], self.rstd[:, c:c + 1],
                                    op0=ALU.add, op1=ALU.mult).then_inc(s, 1))
            xn_done[tc] = self.at("dv")
            if tc == 0:
                self.wait("t", "cast")
            self.wait("t", "dv", xn_done[tc])
            if tc >= 2:
                self.wait("t", "dv", ev_done[tc - 2])
            ps = self.PS[b].bitcast(BF16)
            for dj in range(NDJ):
                if dj == NDJ - 1:
                    self.bump("pe")
                    self.st("t", lambda e, r, dj=dj, ps=ps, s=spe:
                            e.transpose(ps[:, dj * 128:(dj + 1) * 128],
                                        self.xnb[:, dj * 128:(dj + 1) * 128],
                                        self.sid[:, :]).then_inc(s, 1))
                else:
                    self.st("t", lambda e, r, dj=dj, ps=ps:
                            e.transpose(ps[:, dj * 128:(dj + 1) * 128],
                                        self.xnb[:, dj * 128:(dj + 1) * 128],
                                        self.sid[:, :]))
            self.wait("v", "pe")
            self.bump("dv")
            dst = self.bigA[:, :].rearrange("p (a c) -> p a c", a=NDJ)[:, :, tc * 128:(tc + 1) * 128]
            self.st("v", lambda e, r, ps=ps, dst=dst, s=sdv:
                    e.tensor_copy(dst, ps[:, 0:D].rearrange("p (a c) -> p a c", a=NDJ)).then_inc(s, 1))
            ev_done[tc] = self.at("dv")
        return ev_done[NTC - 1]

    def emit_fproj(self, w, bias_col0, scale, dest_kind, attn_idx=0, arena_ready=0):
        gpm = self.need_w(w)
        sdv, spe = self.sem["dv"], self.sem["pe"]
        mm_done, ev_done, st_done = {}, {}, {}
        for dog in range(16):
            b = dog % 2
            if dog >= 2:
                self.wait("s", "pe", mm_done[dog - 2])
            src = self.wbf[:, w * D + dog * 256:w * D + (dog + 1) * 256].rearrange("(a p) c -> p a c", p=128)
            dsl = self.wsl[:, b * NDJ * 512:(b + 1) * NDJ * 512] \
                .rearrange("p (a c) -> p a c", c=512)[:, :, 0:256]
            if dog == 0:
                self.wait("s", "pe")
                if arena_ready:
                    self.wait("s", "dv", arena_ready)
            self.dma("s", dsl, src, "lw0" if b == 0 else "lw1")
            self.wait("t", "lw0" if b == 0 else "lw1")
            self.wait("t", "gp", gpm)
            if dog == 0:
                self.wait("t", "dv")
                self.wait("t", "ac")
            for doc in range(2):
                do = dog * 2 + doc
                ps = self.PS[do % 2][:, 0:1024]
                if do >= 2:
                    self.wait("t", "dv", ev_done[do - 2])
                for nh in range(2):
                    for di in range(NDJ):
                        lt = self.wsl[:, b * NDJ * 512 + di * 512 + doc * 128:
                                      b * NDJ * 512 + di * 512 + doc * 128 + 128]
                        rh = self.bigA[:, di * TOK + nh * 512: di * TOK + nh * 512 + 512]
                        last = (nh == 1 and di == NDJ - 1)
                        if last:
                            self.bump("pe")
                            self.st("t", lambda e, r, lt=lt, rh=rh, ps=ps, nh=nh, di=di, s=spe:
                                    e.matmul(ps[:, nh * 512:(nh + 1) * 512], lt, rh,
                                             start=(di == 0), stop=(di == NDJ - 1)).then_inc(s, 1))
                        else:
                            self.st("t", lambda e, r, lt=lt, rh=rh, ps=ps, nh=nh, di=di:
                                    e.matmul(ps[:, nh * 512:(nh + 1) * 512], lt, rh,
                                             start=(di == 0), stop=(di == NDJ - 1)))
                mm_done[dog] = self.at("pe")
                self.wait("v", "pe")
                if do >= 2:
                    self.wait("v", "sk0" if do % 2 == 0 else "sk1")
                else:
                    self.waitall("v", ["sk0", "sk1", "sx0", "sx1"])
                sdst = self.stg[:, (do % 2) * 1024:(do % 2 + 1) * 1024]
                bcol = self.sbias[:, bias_col0 + do:bias_col0 + do + 1]
                self.bump("dv")
                if scale is None:
                    self.st("v", lambda e, r, ps=ps, d=sdst, bc=bcol, s=sdv:
                            e.tensor_scalar(d, ps, bc, None, op0=ALU.add).then_inc(s, 1))
                else:
                    self.st("v", lambda e, r, ps=ps, d=sdst, bc=bcol, sc=scale, s=sdv:
                            e.tensor_scalar(d, ps, bc, sc, op0=ALU.add, op1=ALU.mult).then_inc(s, 1))
                ev_done[do] = self.at("dv")
                self.wait("s", "dv", ev_done[do])
                sks = "sk0" if do % 2 == 0 else "sk1"
                if dest_kind == "qx":
                    self.dma("s", self.qx[do * 128:(do + 1) * 128, :], sdst, sks)
                else:
                    self.dma("s", self.kl[do * 128:(do + 1) * 128, :], sdst, sks)
                st_done[do] = self.at(sks)
        return 0

    def emit_tproj(self, w, lhsT_arena, bias_bc_idx, out_kind, attn_idx=0,
                   with_lora=False, src_x=None, dst_x=None, arena_ready=None):
        gpm = self.need_w(w)
        sdv, spe = self.sem["dv"], self.sem["pe"]
        bb_ld = {}
        mm_done, ev_done, st_done = {}, {}, {}
        for dg in range(8):
            b = dg % 2
            if dg >= 2:
                self.wait("s", "pe", mm_done[dg - 2])
            src = self.wbf[:, w * D + dg * 512:w * D + (dg + 1) * 512].rearrange("(a p) c -> p a c", p=128)
            dsl = self.wsl[:, b * NDJ * 512:(b + 1) * NDJ * 512].rearrange("p (a c) -> p a c", c=512)
            if dg == 0:
                self.wait("s", "pe")
                if arena_ready:
                    self.wait("s", "dv", arena_ready)
            self.dma("s", dsl, src, "lw0" if b == 0 else "lw1")
            if dg == 0:
                self.wait("t", "dv")
                self.wait("t", "ac")
            if bias_bc_idx is not None:
                if dg >= 2:
                    self.wait("g", "dv", ev_done[(dg - 2) * NTC + NTC - 1])
                self.dma("g", self.sbb[:, b * 512:(b + 1) * 512],
                         self.bbcast[bias_bc_idx][:, dg * 512:(dg + 1) * 512], "ldb")
                bb_ld[dg] = self.at("ldb")
            self.wait("t", "lw0" if b == 0 else "lw1")
            self.wait("t", "gp", gpm)
            if arena_ready:
                self.wait("t", "dv", arena_ready)
                arena_ready = None
            for tc in range(NTC):
                it = dg * NTC + tc
                ps = self.PS[it % 2][:, 0:512]
                if it >= 2:
                    self.wait("t", "dv", ev_done[it - 2])
                for di in range(NDJ):
                    lt = lhsT_arena[:, di * TOK + tc * 128: di * TOK + tc * 128 + 128]
                    rh = self.wsl[:, b * NDJ * 512 + di * 512:b * NDJ * 512 + (di + 1) * 512]
                    last = (di == NDJ - 1) and not with_lora
                    if last:
                        self.bump("pe")
                        self.st("t", lambda e, r, lt=lt, rh=rh, ps=ps, di=di, s=spe:
                                e.matmul(ps, lt, rh, start=(di == 0), stop=True).then_inc(s, 1))
                    else:
                        self.st("t", lambda e, r, lt=lt, rh=rh, ps=ps, di=di, wl=with_lora:
                                e.matmul(ps, lt, rh, start=(di == 0), stop=False if wl else (di == NDJ - 1)))
                if with_lora:
                    self.bump("pe")
                    self.st("t", lambda e, r, ps=ps, tc=tc, dg=dg, s=spe:
                            e.matmul(ps, self.szm[:, tc * 128:(tc + 1) * 128],
                                     self.sbcat[:, dg * 512:(dg + 1) * 512],
                                     start=False, stop=True).then_inc(s, 1))
                mm_done[dg] = self.at("pe")
                if out_kind == "vx":
                    self.wait("v", "pe")
                    self.wait("v", "ldb", bb_ld[dg])
                    if it >= 2:
                        self.wait("v", "sk0" if it % 2 == 0 else "sk1")
                    else:
                        self.waitall("v", ["sk0", "sk1", "sx0", "sx1"])
                    sdst = self.stg[:, (it % 2) * 1024:(it % 2) * 1024 + 512]
                    self.bump("dv")
                    self.st("v", lambda e, r, ps=ps, d=sdst, b=b, s=sdv:
                            e.scalar_tensor_tensor(d, ps, 1.0, self.sbb[:, b * 512:(b + 1) * 512],
                                                   op0=ALU.mult, op1=ALU.add).then_inc(s, 1))
                    ev_done[it] = self.at("dv")
                    self.wait("s", "dv", ev_done[it])
                    sks = "sk0" if it % 2 == 0 else "sk1"
                    self.dma("s", self.vl[tc * 128:(tc + 1) * 128, dg * 512:(dg + 1) * 512], sdst, sks)
                    st_done[it] = self.at(sks)
                else:
                    if it >= 2:
                        self.wait("s", "dv", ev_done[it - 2])
                    self.dma("s", self.xsl[:, (it % 2) * 512:(it % 2 + 1) * 512],
                             src_x[tc * 128:(tc + 1) * 128, dg * 512:(dg + 1) * 512],
                             "lx0" if it % 2 == 0 else "lx1")
                    self.wait("v", "pe")
                    self.wait("v", "lx0" if it % 2 == 0 else "lx1")
                    if it >= 2:
                        self.wait("v", "sx0" if it % 2 == 0 else "sx1")
                    else:
                        self.waitall("v", ["sk0", "sk1", "sx0", "sx1"])
                    sdst = self.stgf[:, (it % 2) * 512:(it % 2 + 1) * 512]
                    if bias_bc_idx is not None:
                        self.wait("v", "ldb", bb_ld[dg])
                        self.bump("dv")
                        self.st("v", lambda e, r, ps=ps, b=b, s=sdv:
                                e.scalar_tensor_tensor(ps, ps, 1.0, self.sbb[:, b * 512:(b + 1) * 512],
                                                       op0=ALU.mult, op1=ALU.add).then_inc(s, 1))
                        self.wait("v", "dv")
                    self.bump("dv")
                    self.st("v", lambda e, r, ps=ps, d=sdst, it=it, s=sdv:
                            e.tensor_tensor(d, ps, self.xsl[:, (it % 2) * 512:(it % 2 + 1) * 512],
                                            op=ALU.add).then_inc(s, 1))
                    ev_done[it] = self.at("dv")
                    self.wait("s", "dv", ev_done[it])
                    sxs = "sx0" if it % 2 == 0 else "sx1"
                    self.dma("s", dst_x[tc * 128:(tc + 1) * 128, dg * 512:(dg + 1) * 512], sdst, sxs)
                    st_done[it] = self.at(sxs)
        return 0

    def emit_attention(self, kind, kv_gp, only_first=0):
        sdv, sac, spe = self.sem["dv"], self.sem["ac"], self.sem["pe"]
        its = []
        if kind == "a":
            for h in range(32):
                for j in range(NTC):
                    its.append((h, j, (2 * j + 2) * 128))
        else:
            for h in range(16):
                for j in range(NTC):
                    its.append((h, j, 2048))
        n_it = len(its)
        aev_done = {}
        exp_done = {}
        ptev_done = {}
        rl_done = {}
        qk_done = {}
        tr_done = {}
        pv_done = {}
        kq_ld = {}
        v_ld = {}
        qe_done = {}

        def emit_head_loads(h):
            hb = h % 2
            if kind == "a":
                hp, hh = h // 2, h % 2
                lvs = "lv0" if hp % 2 == 0 else "lv1"
                lks = "lk0" if hb == 0 else "lk1"
                if hh == 0:
                    if hp >= 2:
                        self.wait("s", "pe", pv_done.get((hp - 2) * 16 + 15, 0))
                    for c in range(16):
                        self.dma("s", self.vtile[:, (hp % 2) * 4096 + c * 256:(hp % 2) * 4096 + (c + 1) * 256],
                                 self.vx[0][(c % 2) * TOK + (c // 2) * 128:(c % 2) * TOK + (c // 2) * 128 + 128,
                                            hp * 256:hp * 256 + 256], lvs)
                    v_ld[hp] = (lvs, self.at(lvs))
                if h >= 2:
                    self.wait("s", "pe", qk_done.get((h - 2) * 8 + 7, 0))
                for par in range(2):
                    self.dma("s", self.ktile[:, hb * 4096 + par * 1024:hb * 4096 + (par + 1) * 1024],
                             self.kx[0][par * D + h * 128:par * D + (h + 1) * 128, :], lks)
                self.dma("s", self.qtile[:, hb * 2048:hb * 2048 + 1024],
                         self.qx[h * 128:(h + 1) * 128, :], lks)
                kq_ld[h] = (lks, self.at(lks))
            else:
                lvs = "lv0" if hb == 0 else "lv1"
                lks = "lk0" if hb == 0 else "lk1"
                if h >= 2:
                    self.wait("s", "pe", qk_done.get((h - 2) * 8 + 7, 0))
                    self.wait("s", "pe", pv_done.get((h - 2) * 8 + 7, 0))
                for c in range(16):
                    self.dma("s", self.vtile[:, hb * 4096 + c * 256:hb * 4096 + (c + 1) * 256],
                             self.vx[1][(c // 8) * TOK + (c % 8) * 128:(c // 8) * TOK + (c % 8) * 128 + 128,
                                        h * 256:h * 256 + 256], lvs)
                for hdc in range(2):
                    for par in range(2):
                        self.dma("s", self.ktile[:, hb * 4096 + hdc * 2048 + par * 1024:
                                                 hb * 4096 + hdc * 2048 + (par + 1) * 1024],
                                 self.kx[1][par * D + h * 256 + hdc * 128:par * D + h * 256 + hdc * 128 + 128, :],
                                 lks)
                for hdc in range(2):
                    self.dma("s", self.qtile[:, hb * 2048 + hdc * 1024:hb * 2048 + (hdc + 1) * 1024],
                             self.qx[h * 256 + hdc * 128:h * 256 + hdc * 128 + 128, :], lks)
                kq_ld[h] = (lks, self.at(lks))
                v_ld[h] = (lvs, self.at(lvs))

        self.waitall("s", ["sk0", "sk1", "sex"])
        self.wait("t", "gp", kv_gp)
        self.wait("t", "dv")
        self.wait("t", "ac")

        def pe_emit_qk(it_idx):
            h, j, kl = its[it_idx]
            R = self.PS[it_idx % 2]
            if it_idx >= 2:
                self.wait("t", "dv", aev_done[it_idx - 2])
            else:
                self.wait("t", "dv", self.psum_scrub)
            self.wait("t", kq_ld[h][0], kq_ld[h][1])
            if kind == "a":
                nko = (j + 1) * 128
                hb = h % 2
                lt = self.qtile[:, hb * 2048 + j * 128: hb * 2048 + (j + 1) * 128]
                nmm = []
                for par in range(2):
                    base = par * 1024
                    o = 0
                    while o < nko:
                        n = min(512, nko - o)
                        nmm.append((base + o, n, par, o))
                        o += n
                for i2, (co, n, par, o) in enumerate(nmm):
                    last = i2 == len(nmm) - 1
                    rh = self.ktile[:, hb * 4096 + co: hb * 4096 + co + n]
                    if last:
                        self.bump("pe")
                        self.st("t", lambda e, r, R=R, co=co, n=n, lt=lt, rh=rh, s=spe:
                                e.matmul(R[:, co:co + n], lt, rh, start=True, stop=True).then_inc(s, 1))
                    else:
                        self.st("t", lambda e, r, R=R, co=co, n=n, lt=lt, rh=rh:
                                e.matmul(R[:, co:co + n], lt, rh, start=True, stop=True))
            else:
                hb = h % 2
                self.wait("t", "ac", qe_done[h])
                for co in range(0, 2048, 512):
                    for hdc in range(2):
                        lt = self.sqeff[:, hb * 2048 + hdc * 1024 + j * 128:
                                        hb * 2048 + hdc * 1024 + (j + 1) * 128]
                        rh = self.ktile[:, hb * 4096 + hdc * 2048 + co: hb * 4096 + hdc * 2048 + co + 512]
                        fin = co == 1536 and hdc == 1
                        if fin:
                            self.bump("pe")
                            self.st("t", lambda e, r, R=R, co=co, lt=lt, rh=rh, hdc=hdc, s=spe:
                                    e.matmul(R[:, co:co + 512], lt, rh, start=(hdc == 0),
                                             stop=(hdc == 1)).then_inc(s, 1))
                        else:
                            self.st("t", lambda e, r, R=R, co=co, lt=lt, rh=rh, hdc=hdc:
                                    e.matmul(R[:, co:co + 512], lt, rh, start=(hdc == 0), stop=(hdc == 1)))
            qk_done[it_idx] = self.at("pe")

        def ve_softmax(it_idx):
            h, j, kl = its[it_idx]
            R = self.PS[it_idx % 2]
            rn = it_idx % 4
            self.wait("v", "pe", qk_done[it_idx])
            if kind == "a":
                self.bump("dv")
                self.st("v", lambda e, r, R=R, j=j, s=sdv:
                        e.tensor_tensor(R[:, j * 128:(j + 1) * 128], R[:, j * 128:(j + 1) * 128],
                                        self.scm[:, 0:128], op=ALU.add).then_inc(s, 1))
                self.bump("dv")
                self.st("v", lambda e, r, R=R, j=j, s=sdv:
                        e.tensor_tensor(R[:, 1024 + j * 128:1024 + (j + 1) * 128],
                                        R[:, 1024 + j * 128:1024 + (j + 1) * 128],
                                        self.scm[:, 128:256], op=ALU.add).then_inc(s, 1))
                self.wait("v", "dv")
                nko = (j + 1) * 128
                self.bump("dv")
                self.st("v", lambda e, r, R=R, n=nko, rn=rn, s=sdv:
                        e.tensor_reduce(self.sneg[:, rn:rn + 1], R[:, 0:n], axis=AX.X,
                                        op=ALU.max, negate=True).then_inc(s, 1))
                self.bump("dv")
                self.st("v", lambda e, r, R=R, n=nko, rn=rn, s=sdv:
                        e.tensor_reduce(self.sneg[:, 4 + rn:5 + rn], R[:, 1024:1024 + n], axis=AX.X,
                                        op=ALU.max, negate=True).then_inc(s, 1))
                self.wait("v", "dv")
                self.bump("dv")
                self.st("v", lambda e, r, rn=rn, s=sdv:
                        e.tensor_tensor(self.sneg[:, rn:rn + 1], self.sneg[:, rn:rn + 1],
                                        self.sneg[:, 4 + rn:5 + rn], op=ALU.min).then_inc(s, 1))
            else:
                self.bump("dv")
                self.st("v", lambda e, r, R=R, rn=rn, s=sdv:
                        e.tensor_reduce(self.sneg[:, rn:rn + 1], R[:, :], axis=AX.X,
                                        op=ALU.max, negate=True).then_inc(s, 1))
            mx_done = self.at("dv")
            self.wait("a", "dv", mx_done)
            pb = self.spp[:, (it_idx % 2) * 2048:(it_idx % 2 + 1) * 2048]
            if kind == "a":
                nko = (j + 1) * 128
                self.bump("ac")
                self.st("a", lambda e, r, R=R, pb=pb, n=nko, rn=rn, s=sac:
                        e.activation(pb[:, 0:n], R[:, 0:n], AF.Exp, bias=self.sneg[:, rn:rn + 1],
                                     scale=1.0, accum_out=self.sneg[:, 8 + rn:9 + rn]).then_inc(s, 1))
                self.st("a", lambda e, r, R=R, pb=pb, n=nko, rn=rn:
                        e.activation(pb[:, 1024:1024 + n], R[:, 1024:1024 + n], AF.Exp,
                                     bias=self.sneg[:, rn:rn + 1], scale=1.0,
                                     accum_out=self.srl[:, rn:rn + 1]))
                self.bump("ac")
                self.st("a", lambda e, r, s=sac:
                        e.activation(self.sjnk[:, 0:1], self.sjnk[:, 1:2], AF.Copy).then_inc(s, 1))
                exp_done[it_idx] = self.at("ac")
                self.wait("v", "ac", exp_done[it_idx])
                self.bump("dv")
                self.st("v", lambda e, r, rn=rn, s=sdv:
                        e.tensor_tensor(self.srl[:, rn:rn + 1], self.srl[:, rn:rn + 1],
                                        self.sneg[:, 8 + rn:9 + rn], op=ALU.add).then_inc(s, 1))
                self.wait("v", "dv")
                self.bump("dv")
                self.st("v", lambda e, r, rn=rn, s=sdv:
                        e.reciprocal(self.srl[:, rn:rn + 1], self.srl[:, rn:rn + 1]).then_inc(s, 1))
            else:
                self.st("a", lambda e, r, R=R, pb=pb, rn=rn:
                        e.activation(pb[:, :], R[:, :], AF.Exp, bias=self.sneg[:, rn:rn + 1],
                                     scale=1.0, accum_out=self.srl[:, rn:rn + 1]))
                self.bump("ac")
                self.st("a", lambda e, r, s=sac:
                        e.activation(self.sjnk[:, 0:1], self.sjnk[:, 1:2], AF.Copy).then_inc(s, 1))
                exp_done[it_idx] = self.at("ac")
                self.wait("v", "ac", exp_done[it_idx])
                self.bump("dv")
                self.st("v", lambda e, r, rn=rn, s=sdv:
                        e.reciprocal(self.srl[:, rn:rn + 1], self.srl[:, rn:rn + 1]).then_inc(s, 1))
            rl_done[it_idx] = self.at("dv")

        def pe_emit_trans(it_idx):
            h, j, kl = its[it_idx]
            R = self.PS[it_idx % 2]
            Rb = R.bitcast(BF16)
            pb = self.spp[:, (it_idx % 2) * 2048:(it_idx % 2 + 1) * 2048]
            self.wait("t", "ac", exp_done[it_idx])
            nch = (2 * j + 2) if kind == "a" else 16
            for c in range(nch):
                if kind == "a":
                    scol = (c % 2) * 1024 + (c // 2) * 128
                else:
                    scol = c * 128
                last = c == nch - 1
                if last:
                    self.bump("pe")
                    self.st("t", lambda e, r, Rb=Rb, c=c, pb=pb, scol=scol, s=spe:
                            e.transpose(Rb[:, c * 128:(c + 1) * 128], pb[:, scol:scol + 128],
                                        self.sid[:, :]).then_inc(s, 1))
                else:
                    self.st("t", lambda e, r, Rb=Rb, c=c, pb=pb, scol=scol:
                            e.transpose(Rb[:, c * 128:(c + 1) * 128], pb[:, scol:scol + 128],
                                        self.sid[:, :]))
            tr_done[it_idx] = self.at("pe")
            nb = nch * 128
            ptb = self.spt[:, (it_idx % 2) * 2048:(it_idx % 2) * 2048 + nb]
            if it_idx % 2 == 0:
                self.wait("v", "pe", tr_done[it_idx])
                self.bump("dv")
                self.st("v", lambda e, r, Rb=Rb, d=ptb, nb=nb, s=sdv:
                        e.tensor_copy(d, Rb[:, 0:nb]).then_inc(s, 1))
                ptev_done[it_idx] = ("dv", self.at("dv"))
            else:
                self.wait("a", "pe", tr_done[it_idx])
                self.bump("ac")
                self.st("a", lambda e, r, Rb=Rb, d=ptb, nb=nb, s=sac:
                        e.activation(d, Rb[:, 0:nb], AF.Copy).then_inc(s, 1))
                ptev_done[it_idx] = ("ac", self.at("ac"))

        def pe_emit_pv(it_idx):
            h, j, kl = its[it_idx]
            R = self.PS[it_idx % 2]
            sk, sv_ = ptev_done[it_idx]
            self.wait("t", sk, sv_)
            nch = (2 * j + 2) if kind == "a" else 16
            hd = 128 if kind == "a" else 256
            if kind == "a":
                hp, hh = h // 2, h % 2
                self.wait("t", v_ld[hp][0], v_ld[hp][1])
                vb = (hp % 2) * 4096
                voff = hh * 128
            else:
                self.wait("t", v_ld[h][0], v_ld[h][1])
                vb = (h % 2) * 4096
                voff = 0
            pvp = R[:, 1536:1536 + hd]
            for c in range(nch):
                lt = self.spt[:, (it_idx % 2) * 2048 + c * 128:(it_idx % 2) * 2048 + (c + 1) * 128]
                rh = self.vtile[:, vb + c * 256 + voff: vb + c * 256 + voff + hd]
                last = c == nch - 1
                if last:
                    self.bump("pe")
                    self.st("t", lambda e, r, pvp=pvp, lt=lt, rh=rh, c=c, nch=nch, s=spe:
                            e.matmul(pvp, lt, rh, start=(c == 0), stop=(c == nch - 1)).then_inc(s, 1))
                else:
                    self.st("t", lambda e, r, pvp=pvp, lt=lt, rh=rh, c=c, nch=nch:
                            e.matmul(pvp, lt, rh, start=(c == 0), stop=(c == nch - 1)))
            pv_done[it_idx] = self.at("pe")
            self.wait("v", "pe", pv_done[it_idx])
            self.wait("v", "dv", rl_done[it_idx])
            rn = it_idx % 4
            dcol = j * D + h * hd
            self.bump("dv")
            self.st("v", lambda e, r, pvp=pvp, dcol=dcol, hd=hd, rn=rn, s=sdv:
                    e.tensor_scalar(self.bigB[:, dcol:dcol + hd], pvp, self.srl[:, rn:rn + 1],
                                    None, op0=ALU.mult).then_inc(s, 1))
            aev_done[it_idx] = self.at("dv")

        def emit_qeff(h):
            it_idx = h * 8
            hb = h % 2
            R = self.PS[it_idx % 2]
            if it_idx >= 2:
                self.wait("t", "dv", aev_done[it_idx - 2])
            else:
                self.wait("t", "dv", self.psum_scrub)
            self.wait("t", kq_ld[h][0], kq_ld[h][1])
            self.wait("t", self.mh_done[0], self.mh_done[1])
            for hdc in range(2):
                for nh in range(2):
                    last = hdc == 1 and nh == 1
                    lt = self.smh[:, h * 128:(h + 1) * 128]
                    rh = self.qtile[:, hb * 2048 + hdc * 1024 + nh * 512:
                                    hb * 2048 + hdc * 1024 + (nh + 1) * 512]
                    if last:
                        self.bump("pe")
                        self.st("t", lambda e, r, R=R, hdc=hdc, nh=nh, lt=lt, rh=rh, s=spe:
                                e.matmul(R[:, hdc * 1024 + nh * 512:hdc * 1024 + (nh + 1) * 512],
                                         lt, rh, start=True, stop=True).then_inc(s, 1))
                    else:
                        self.st("t", lambda e, r, R=R, hdc=hdc, nh=nh, lt=lt, rh=rh:
                                e.matmul(R[:, hdc * 1024 + nh * 512:hdc * 1024 + (nh + 1) * 512],
                                         lt, rh, start=True, stop=True))
            qe_pe = self.at("pe")
            self.wait("a", "pe", qe_pe)
            self.bump("ac")
            self.st("a", lambda e, r, R=R, hb=hb, s=sac:
                    e.activation(self.sqeff[:, hb * 2048:(hb + 1) * 2048], R[:, :], AF.Copy).then_inc(s, 1))
            qe_done[h] = self.at("ac")

        heads = 32 if kind == "a" else 16
        if only_first:
            n = abs(only_first)
            loads_only_tail = only_first < 0
            nh_full = (n + 7) // 8
            for h in range(nh_full):
                emit_head_loads(h)
                if kind == "b":
                    emit_qeff(h)
                for j in range(min(8, n - h * 8)):
                    it_idx = h * 8 + j
                    pe_emit_qk(it_idx)
                    ve_softmax(it_idx)
                    if it_idx > 0:
                        pe_emit_trans(it_idx - 1)
                        pe_emit_pv(it_idx - 1)
            pe_emit_trans(n - 1)
            pe_emit_pv(n - 1)
            if loads_only_tail:
                emit_head_loads(nh_full)
            return aev_done[n - 1]
        for h in range(heads):
            emit_head_loads(h)
            if kind == "b":
                emit_qeff(h)
            for j in range(NTC):
                it_idx = h * 8 + j
                pe_emit_qk(it_idx)
                ve_softmax(it_idx)
                if it_idx > 0:
                    pe_emit_trans(it_idx - 1)
                    pe_emit_pv(it_idx - 1)
        pe_emit_trans(n_it - 1)
        pe_emit_pv(n_it - 1)
        return aev_done[n_it - 1]

    def dump_sbuf(self, ap_bf16):
        self.wait("g", "dv")
        self.wait("g", "ac")
        self.wait("g", "pe")
        ydst = self.yc.rearrange("a b -> (a b)").rearrange("(p c) -> p c", p=128)
        self.dma("g", ydst, ap_bf16, "dbg")
        self.wait("s", "dbg")

    def dump_dram(self, src_ap, rows):
        self.wait("g", "dv")
        self.waitall("g", ["sk0", "sk1", "sx0", "sx1", "sex"])
        ydst = self.yc.rearrange("a b -> (a b)")[0:rows * 1024].rearrange("(a c) -> a c", c=1024)
        self.dma("g", ydst, src_ap, "dbg")
        self.wait("s", "dbg")

    def sublayer_attn_a(self, src, dst):
        if DEBUG_STOP == "prelude":
            self.dump_sbuf(self.bigA[:, :])
            return
        ln_ev = self.emit_ln(src)
        if DEBUG_STOP == "ln":
            self.wait("g", "dv", ln_ev)
            self.dump_sbuf(self.bigA[:, :])
            return
        self.emit_fproj(W_QA, 0, C_A, "qx", arena_ready=ln_ev)
        if DEBUG_STOP == "qproj":
            self.dump_dram(self.qx.rearrange("a b -> (a b)").rearrange("(a c) -> a c", c=1024), 4096)
            return
        self.emit_fproj(W_KA, 32, None, "kx", attn_idx=0)
        self.emit_tproj(W_VA, self.bigA, 0, "vx", attn_idx=0)
        if DEBUG_STOP == "kv":
            self.dump_dram(self.vl.rearrange("a b -> (a b)").rearrange("(a c) -> a c", c=1024), 4096)
            return
        kv_gp = self.kv_exchange(0, FLAG_KVA)
        if isinstance(DEBUG_STOP, str) and DEBUG_STOP.startswith("attnN"):
            n = int(DEBUG_STOP[5:])
            a_done = self.emit_attention("a", kv_gp, only_first=n)
            self.wait("g", "dv", a_done)
            self.waitall("g", ["lk0", "lk1", "lv0", "lv1"])
            self.dump_sbuf(self.bigB[:, :])
            return
        if DEBUG_STOP == "exch":
            self.wait("g", "gp", kv_gp)
            self.dump_dram(self.vx[0].rearrange("a b -> (a b)")[0:TOK * D].rearrange("(a c) -> a c", c=1024), 4096)
            return
        a_done = self.emit_attention("a", kv_gp)
        if DEBUG_STOP == "attn":
            self.wait("g", "dv", a_done)
            self.dump_sbuf(self.bigB[:, :])
            return
        sdv, spe = self.sem["dv"], self.sem["pe"]
        ev_done = {}
        self.wait("t", "dv", a_done)
        for tc in range(NTC):
            ps = self.PS[tc % 2].bitcast(BF16)
            if tc >= 2:
                self.wait("t", "dv", ev_done[tc - 2])
            for dj in range(NDJ):
                last = dj == NDJ - 1
                srcb = self.bigB[:, tc * D + dj * 128: tc * D + (dj + 1) * 128]
                if last:
                    self.bump("pe")
                    self.st("t", lambda e, r, ps=ps, dj=dj, srcb=srcb, s=spe:
                            e.transpose(ps[:, dj * 128:(dj + 1) * 128], srcb, self.sid[:, :]).then_inc(s, 1))
                else:
                    self.st("t", lambda e, r, ps=ps, dj=dj, srcb=srcb:
                            e.transpose(ps[:, dj * 128:(dj + 1) * 128], srcb, self.sid[:, :]))
            self.wait("v", "pe")
            self.bump("dv")
            dstA = self.bigA[:, :].rearrange("p (a c) -> p a c", a=NDJ)[:, :, tc * 128:(tc + 1) * 128]
            self.st("v", lambda e, r, ps=ps, dstA=dstA, s=sdv:
                    e.tensor_copy(dstA, ps[:, 0:D].rearrange("p (a c) -> p a c", a=NDJ)).then_inc(s, 1))
            ev_done[tc] = self.at("dv")
        self.emit_tproj(W_OA, self.bigA, 1, "addx", src_x=src, dst_x=dst,
                        arena_ready=ev_done[NTC - 1])

    def sublayer_ffn(self, src, dst):
        ln_ev = self.emit_ln(src)
        sdv, sac, spe = self.sem["dv"], self.sem["ac"], self.sem["pe"]
        self.wait("t", "cast", self.cast_pre)
        self.wait("t", "dv", ln_ev)
        self.wait("g", "dv")
        self.dma("g", self.sbcat[:, :], self.bcatI[:, :], "ldb")
        self.dma("g", self.sacat[:, :].rearrange("p (a c) -> p a c", a=NDJ),
                 self.acatI.rearrange("(a p) c -> p a c", p=128), "ldb")
        self.wait("t", "ldb")
        glog_done = {}
        for tc in range(NTC):
            ps = self.PS[0][:, tc * 8:tc * 8 + 4]
            for di in range(NDJ):
                lt = self.bigA[:, di * TOK + tc * 128: di * TOK + tc * 128 + 128]
                rh = self.sgw[:, di * 4:(di + 1) * 4]
                last = di == NDJ - 1
                if last:
                    self.bump("pe")
                    self.st("t", lambda e, r, ps=ps, lt=lt, rh=rh, di=di, s=spe:
                            e.matmul(ps, lt, rh, start=(di == 0), stop=(di == NDJ - 1)).then_inc(s, 1))
                else:
                    self.st("t", lambda e, r, ps=ps, lt=lt, rh=rh, di=di:
                            e.matmul(ps, lt, rh, start=(di == 0), stop=(di == NDJ - 1)))
            glog_done[tc] = self.at("pe")
        zps = self.PS[1]
        for nh in range(2):
            for di in range(NDJ):
                lt = self.sacat[:, di * 32:(di + 1) * 32]
                rh = self.bigA[:, di * TOK + nh * 512: di * TOK + (nh + 1) * 512]
                last = di == NDJ - 1
                if last:
                    self.bump("pe")
                    self.st("t", lambda e, r, zps=zps, lt=lt, rh=rh, nh=nh, di=di, s=spe:
                            e.matmul(zps[0:32, nh * 512:(nh + 1) * 512], lt, rh,
                                     start=(di == 0), stop=(di == NDJ - 1)).then_inc(s, 1))
                else:
                    self.st("t", lambda e, r, zps=zps, lt=lt, rh=rh, nh=nh, di=di:
                            e.matmul(zps[0:32, nh * 512:(nh + 1) * 512], lt, rh,
                                     start=(di == 0), stop=(di == NDJ - 1)))
        z_done = self.at("pe")
        oh_done = {}
        for tc in range(NTC):
            self.wait("v", "pe", glog_done[NTC - 1])
            self.wait("v", "ldc")
            ps = self.PS[0][:, tc * 8:tc * 8 + 4]
            self.bump("dv")
            self.st("v", lambda e, r, ps=ps, tc=tc, s=sdv:
                    e.tensor_tensor(self.sglog[:, tc * 4:(tc + 1) * 4], ps, self.seb[:, :],
                                    op=ALU.add).then_inc(s, 1))
            self.wait("v", "dv")
            self.bump("dv")
            self.st("v", lambda e, r, tc=tc, s=sdv:
                    e.tensor_reduce(self.sgmax[:, tc:tc + 1], self.sglog[:, tc * 4:(tc + 1) * 4],
                                    axis=AX.X, op=ALU.max).then_inc(s, 1))
            self.wait("v", "dv")
            self.bump("dv")
            self.st("v", lambda e, r, tc=tc, s=sdv:
                    e.tensor_scalar(self.soh[:, tc * 4:(tc + 1) * 4], self.sglog[:, tc * 4:(tc + 1) * 4],
                                    self.sgmax[:, tc:tc + 1], 1.0, op0=ALU.is_equal,
                                    op1=ALU.mult).then_inc(s, 1))
            oh_done[tc] = self.at("dv")
        ohT_ps = self.PS[0].bitcast(BF16)
        for tc in range(NTC):
            self.wait("t", "dv", oh_done[tc])
            last = tc == NTC - 1
            if last:
                self.bump("pe")
                self.st("t", lambda e, r, tc=tc, s=spe:
                        e.transpose(ohT_ps[0:4, 2048 + tc * 128:2048 + (tc + 1) * 128],
                                    self.soh[:, tc * 4:(tc + 1) * 4], self.sid[:, :]).then_inc(s, 1))
            else:
                self.st("t", lambda e, r, tc=tc:
                        e.transpose(ohT_ps[0:4, 2048 + tc * 128:2048 + (tc + 1) * 128],
                                    self.soh[:, tc * 4:(tc + 1) * 4], self.sid[:, :]))
        ohT_done = self.at("pe")
        self.wait("v", "pe", ohT_done)
        self.bump("dv")
        self.st("v", lambda e, r, s=sdv:
                e.tensor_copy(self.sohT[:, :], ohT_ps[0:4, 2048:2048 + 1024]).then_inc(s, 1))
        self.wait("v", "pe", z_done)
        self.bump("dv")
        self.st("v", lambda e, r, zps=zps, s=sdv:
                e.tensor_copy(self.szT[:, :], zps[0:32, 0:1024]).then_inc(s, 1))
        zev = self.at("dv")
        self.wait("t", "dv", zev)
        for nh in range(2):
            last = nh == 1
            if last:
                self.bump("pe")
                self.st("t", lambda e, r, zps=zps, nh=nh, s=spe:
                        e.matmul(zps[0:32, 1024 + nh * 512:1024 + (nh + 1) * 512],
                                 self.sp48[:, :], self.sohT[0:4, nh * 512:(nh + 1) * 512],
                                 start=True, stop=True).then_inc(s, 1))
            else:
                self.st("t", lambda e, r, zps=zps, nh=nh:
                        e.matmul(zps[0:32, 1024 + nh * 512:1024 + (nh + 1) * 512],
                                 self.sp48[:, :], self.sohT[0:4, nh * 512:(nh + 1) * 512],
                                 start=True, stop=True))
        oh32_done = self.at("pe")
        self.wait("v", "pe", oh32_done)
        self.bump("dv")
        self.st("v", lambda e, r, zps=zps, s=sdv:
                e.tensor_copy(self.soh32[:, :], zps[0:32, 1024:2048]).then_inc(s, 1))
        self.wait("v", "dv")
        self.bump("dv")
        self.st("v", lambda e, r, s=sdv:
                e.tensor_tensor(self.szm[:, :], self.szT[:, :], self.soh32[:, :],
                                op=ALU.mult).then_inc(s, 1))
        zm_done = self.at("dv")
        self.emit_tproj(W_UP, self.bigA, None, "addx", with_lora=True,
                        src_x=src, dst_x=dst, arena_ready=zm_done)

    def sublayer_attn_b(self, src, dst):
        sdv, sac = self.sem["dv"], self.sem["ac"]
        self.wait("s", "dv")
        self.dma("s", self.sd0g[:, :].rearrange("p (a c) -> p a c", a=4),
                 self.d0gI.rearrange("a p c -> p a c"), "ldc")
        self.wait("v", "ldc")
        mh_prev = 0
        for h in range(16):
            for c in range(3):
                self.bump("dv")
                in1 = self.sd0g[:, 0:128] if c == 0 else self.smhf[:, :]
                self.st("v", lambda e, r, h=h, c=c, in1=in1, s=sdv:
                        e.scalar_tensor_tensor(self.smhf[:, :], self.sd0g[:, (c + 1) * 128:(c + 2) * 128],
                                               self.sbw[:, h * 3 + c:h * 3 + c + 1], in1,
                                               op0=ALU.mult, op1=ALU.add).then_inc(s, 1))
                self.wait("v", "dv")
            self.wait("a", "dv")
            self.bump("ac")
            self.st("a", lambda e, r, h=h, s=sac:
                    e.activation(self.smh[:, h * 128:(h + 1) * 128], self.smhf[:, :], AF.Copy).then_inc(s, 1))
            self.wait("v", "ac")
        self.mh_done = ("ac", self.at("ac"))

        if DEBUG_STOP == "x2":
            self.dump_dram(src.rearrange("a b -> (a b)").rearrange("(a c) -> a c", c=1024), 4096)
            return
        ln_ev = self.emit_ln(src)
        if DEBUG_STOP == "lnb":
            self.wait("g", "dv", ln_ev)
            self.dump_sbuf(self.bigA[:, :])
            return
        self.emit_fproj(W_QB, 64, None, "qx", arena_ready=ln_ev)
        self.emit_fproj(W_KB, 96, None, "kx", attn_idx=1)
        self.emit_tproj(W_VB, self.bigA, 2, "vx", attn_idx=1)
        if DEBUG_STOP == "kvb":
            self.dump_dram(self.vl.rearrange("a b -> (a b)").rearrange("(a c) -> a c", c=1024), 4096)
            return
        kv_gp = self.kv_exchange(1, FLAG_KVB)
        a_done = self.emit_attention("b", kv_gp)
        if DEBUG_STOP == "attnb0":
            self.emit_attention("b", kv_gp, only_first=1)
            self.waitall("g", ["lk0", "lk1", "lv0", "lv1"])
            self.wait("g", "dv")
            self.wait("g", "ac")
            self.dump_sbuf(self.arena[:, :])
            return
        if DEBUG_STOP == "attnb":
            self.wait("g", "dv", a_done)
            self.waitall("g", ["lk0", "lk1", "lv0", "lv1"])
            self.dump_sbuf(self.bigB[:, :])
            return
        spe = self.sem["pe"]
        st_done = {}
        self.wait("s", "dv", a_done)
        for tc in range(NTC):
            b = tc % 2
            if tc >= 2:
                self.wait("s", "sx0" if b == 0 else "sx1")
            self.dma("s", self.xadd_in[:, 0:D] if b == 0 else self.xadd_in[:, D:2 * D],
                     src[tc * 128:(tc + 1) * 128, :], "lx0" if b == 0 else "lx1")
            self.wait("v", "lx0" if b == 0 else "lx1")
            self.bump("dv")
            xin = self.xadd_in[:, 0:D] if b == 0 else self.xadd_in[:, D:2 * D]
            xout = self.xadd_out[:, 0:D] if b == 0 else self.xadd_out[:, D:2 * D]
            self.st("v", lambda e, r, xin=xin, xout=xout, tc=tc, s=sdv:
                    e.tensor_tensor(xout, xin, self.bigB[:, tc * D:(tc + 1) * D], op=ALU.add).then_inc(s, 1))
            self.wait("s", "dv")
            sxs = "sx0" if b == 0 else "sx1"
            self.dma("s", dst[tc * 128:(tc + 1) * 128, :], xout, sxs)
            st_done[tc] = self.at(sxs)


def build_nc(stage=STAGE):
    nc = bass.Bass("TRN2", target_bir_lowering=False, debug=False)
    nc.detect_race_conditions = False
    kb = KB(nc, stage)
    return kb.build()


def _tri_mask():
    m = np.zeros((128, 128), dtype=np.float32)
    for q in range(128):
        m[q, q + 1:] = NEG
    return m


def _prep_inputs(inputs):
    x = np.asarray(inputs["L0_attn_a_x"])
    outs = []
    wts = [inputs["L0_attn_a_Wq"], inputs["L0_attn_a_Wk"], inputs["L0_attn_a_Wv"],
           inputs["L0_attn_a_Wo"], inputs["L0_ffn_W_up"], inputs["L0_attn_b_Wq"],
           inputs["L0_attn_b_Wk"], inputs["L0_attn_b_Wv"]]
    wTf = [np.asarray(w).T.astype(np.float32) for w in wts]
    bcols = np.zeros((128, 128), dtype=np.float32)
    for i, nm in enumerate(["L0_attn_a_bq", "L0_attn_a_bk", "L0_attn_b_bq", "L0_attn_b_bk"]):
        bcols[:, i * 32:(i + 1) * 32] = np.asarray(inputs[nm]).reshape(32, 128).T
    bbcast = np.stack([
        np.broadcast_to(np.asarray(inputs["L0_attn_a_bv"]), (128, D)),
        np.broadcast_to(np.asarray(inputs["L0_attn_a_bo"]), (128, D)),
        np.broadcast_to(np.asarray(inputs["L0_attn_b_bv"]), (128, D)),
    ]).astype(np.float32)
    eb = np.broadcast_to(np.asarray(inputs["L0_ffn_expert_bias"]), (128, 4)).astype(np.float32)
    bw = np.broadcast_to(np.asarray(inputs["L0_attn_b_bias_weight"]).reshape(1, 48), (128, 48)).astype(np.float32)
    gwT = np.ascontiguousarray(np.asarray(inputs["L0_ffn_gate_w"]).T.astype(np.float32))
    acatT = np.ascontiguousarray(np.asarray(inputs["L0_ffn_lora_A"]).reshape(32, D).T.astype(np.float32))
    bcatP = np.ascontiguousarray(np.asarray(inputs["L0_ffn_lora_B"]).transpose(0, 2, 1).reshape(32, D).astype(np.float32))
    p48 = np.kron(np.eye(4, dtype=np.float32), np.ones((1, 8), dtype=np.float32))
    ident = np.eye(128, dtype=np.float32)
    d0g = _d0g_consts()
    tri = _tri_mask()
    full = np.full((128, 128), NEG, dtype=np.float32)
    zero = np.zeros((128, 128), dtype=np.float32)
    cm = {0: np.concatenate([tri, full], axis=1),
          1: np.concatenate([zero, tri], axis=1)}
    in_maps = []
    for c in range(8):
        b, p = c // 2, c % 2
        xcb = np.ascontiguousarray(x[b].reshape(16, 128, D)[p::2].reshape(TOK, D).astype(np.float32))
        m = {"xc": xcb, "bcols": bcols, "bbcast": bbcast, "ebI": eb, "bwI": bw,
             "gwTI": gwT, "acatI": acatT, "bcatI": bcatP, "p48I": p48,
             "idI": ident, "cmI": cm[p], "d0gI": d0g,
             "parI": np.array([[p]], dtype=np.int32),
             "magI": MAGICS.reshape(1, 16)}
        for i in range(8):
            m[f"w{i}"] = np.ascontiguousarray(wTf[i][:, p * 2048:(p + 1) * 2048])
        in_maps.append(m)
    return in_maps


def kernel(**inputs):
    in_maps = _prep_inputs(inputs)
    nc = build_nc(STAGE)
    res = run_bass_kernel_spmd(nc, in_maps, core_ids=list(range(8)))
    x = np.asarray(inputs["L0_attn_a_x"])
    out = np.empty((4, 2048, D), dtype=np.float32)
    for c in range(8):
        b, p = c // 2, c % 2
        o = out[b].reshape(16, 128, D)
        o[p::2] = np.asarray(res.results[c]["yc"]).reshape(NTC, 128, D)
    return out


# version 03 (final)
# speedup vs baseline: 1.2497x; 1.2497x over previous
import numpy as np
import ml_dtypes
from contextlib import ExitStack
import concourse.bass as bass
import concourse.mybir as mybir
from concourse.bass_utils import run_bass_kernel_spmd

F32 = mybir.dt.float32
BF16 = mybir.dt.bfloat16
I32 = mybir.dt.int32
AF = mybir.ActivationFunctionType
ALU = mybir.AluOpType
AX = mybir.AxisListType

D = 4096
TOK = 1024
NTC = 8
NDJ = 32
EPS = 1e-5
INVD = 1.0 / D
C_A = float(1.0 / np.sqrt(128.0))
NEG = -3.0e38
STAGE = 4
DEBUG_STOP = None

W_QA, W_KA, W_VA, W_OA, W_UP, W_QB, W_KB, W_VB = range(8)
FLAG_KVA, FLAG_KVB = 8, 9
MAGICS = np.array([1094795585 + 7919 * i for i in range(16)], dtype=np.int32)


def _blade_tables():
    g = [1.0, 1.0, 1.0]
    n = 8
    masks = sorted(range(n), key=lambda m: (bin(m).count('1'), m))
    inv = {m: i for i, m in enumerate(masks)}
    C = np.zeros((n, n, n), dtype=np.float32)
    for i, a in enumerate(masks):
        for j, b2 in enumerate(masks):
            sw, at = (0, a >> 1)
            while at:
                sw += bin(at & b2).count('1')
                at >>= 1
            s = -1.0 if sw & 1 else 1.0
            C[i, j, inv[a ^ b2]] = s
    grades = np.array([bin(m).count('1') for m in masks])
    rev = np.where(grades * (grades - 1) // 2 % 2 == 1, -1.0, 1.0).astype(np.float32)
    return C, rev


def _d0g_consts():
    C, rev = _blade_tables()
    scale = 1.0 / np.sqrt(256.0)
    D0 = np.kron(np.eye(16, dtype=np.float32), np.diag(rev * scale))
    Gs = [np.kron(np.eye(16, dtype=np.float32), rev[:, None] * C[:, :, c]) for c in (4, 5, 6)]
    return np.stack([D0] + Gs).astype(np.float32)


class _DrainInstr:
    def __init__(self, eng, instr):
        self._eng, self._instr = eng, instr

    def then_inc(self, sem, n):
        return self._eng.drain().then_inc(sem, n)


_DRAIN_OPS = {"tensor_scalar", "tensor_tensor", "tensor_copy", "tensor_reduce",
              "reduce_sum", "reduce_max", "reciprocal", "memset",
              "scalar_tensor_tensor", "activation", "bn_stats", "bn_aggr",
              "engine_nop", "tensor_scalar_mul", "tensor_add", "tensor_sub",
              "tensor_mul", "tensor_max", "matmul", "transpose"}


class _DrainProxy:
    def __init__(self, eng):
        self._e = eng

    def __getattr__(self, nm):
        a = getattr(self._e, nm)
        if nm in _DRAIN_OPS:
            def w(*args, **kw):
                return _DrainInstr(self._e, a(*args, **kw))
            return w
        return a


class KB:

    def __init__(self, nc, stage):
        self.nc = nc
        self.stage = stage
        self.steps = {k: [] for k in "tvags"}
        self.cnt = {}
        self.ctx = ExitStack()

    def st(self, engk, fn):
        self.steps[engk].append(fn)

    def at(self, semk):
        return self.cnt.get(semk, 0)

    def bump(self, semk, n=1):
        self.cnt[semk] = self.cnt.get(semk, 0) + n
        return self.cnt[semk]

    def wait(self, engk, semk, val=None):
        v = self.at(semk) if val is None else val
        if v <= 0:
            return
        s = self.sem[semk]
        self.st(engk, lambda e, r, s=s, v=v: e.wait_ge(s, v))

    def inc_of(self, semk):
        return self.sem[semk]

    def waitall(self, engk, semks):
        for k in semks:
            self.wait(engk, k)

    def dma(self, engk, dst, src, semk):
        s = self.sem[semk]
        self.bump(semk, 16)
        self.st(engk, lambda e, r, d=dst, so=src, s=s: e.dma_start(out=d, in_=so).then_inc(s, 16))

    def dma_par(self, engk, flat_dst, rows, cols, pbase, fixed, src, semk, store=True):
        s = self.sem[semk]
        self.bump(semk, 16)
        regn = "gp" if engk == "g" else "prow"

        def f(e, r, flat_dst=flat_dst, rows=rows, cols=cols, pbase=pbase,
              fixed=fixed, src=src, s=s, regn=regn):
            e.reg_load(r[regn], self.spar[0:1, 0:1])
            e.reg_alu(r[regn], r[regn], pbase, ALU.mult)
            e.reg_alu(r[regn], r[regn], fixed, ALU.add)
            view = flat_dst[bass.ds(e.snap(r[regn]), rows), cols] if store else None
            if store:
                e.dma_start(out=view, in_=src).then_inc(s, 16)
            else:
                e.dma_start(out=src, in_=flat_dst[bass.ds(e.snap(r[regn]), rows), cols]).then_inc(s, 16)
        self.st(engk, f)

    def build(self):
        nc = self.nc
        ctx = self.ctx
        self.xc = nc.dram_tensor("xc", [TOK, D], F32, kind="ExternalInput").ap()
        self.wT = [nc.dram_tensor(f"w{i}", [D, 2048], F32, kind="ExternalInput").ap() for i in range(8)]
        self.bcols = nc.dram_tensor("bcols", [128, 128], F32, kind="ExternalInput").ap()
        self.bbcast = nc.dram_tensor("bbcast", [3, 128, D], F32, kind="ExternalInput").ap()
        self.ebI = nc.dram_tensor("ebI", [128, 4], F32, kind="ExternalInput").ap()
        self.bwI = nc.dram_tensor("bwI", [128, 48], F32, kind="ExternalInput").ap()
        self.gwTI = nc.dram_tensor("gwTI", [D, 4], F32, kind="ExternalInput").ap()
        self.acatI = nc.dram_tensor("acatI", [D, 32], F32, kind="ExternalInput").ap()
        self.bcatI = nc.dram_tensor("bcatI", [32, D], F32, kind="ExternalInput").ap()
        self.p48I = nc.dram_tensor("p48I", [4, 32], F32, kind="ExternalInput").ap()
        self.idI = nc.dram_tensor("idI", [128, 128], F32, kind="ExternalInput").ap()
        self.cmI = nc.dram_tensor("cmI", [128, 256], F32, kind="ExternalInput").ap()
        self.d0gI = nc.dram_tensor("d0gI", [4, 128, 128], F32, kind="ExternalInput").ap()
        self.parI = nc.dram_tensor("parI", [1, 1], I32, kind="ExternalInput").ap()
        self.magI = nc.dram_tensor("magI", [1, 16], I32, kind="ExternalInput").ap()
        self.yc = nc.dram_tensor("yc", [TOK, D], F32, kind="ExternalOutput").ap()
        self.xres = [nc.dram_tensor(f"xres{i}", [TOK, D], F32).ap() for i in range(2)]
        self.qx = nc.dram_tensor("qx", [D, TOK], BF16).ap()
        self.kl = nc.dram_tensor("kl", [D, TOK], BF16).ap()
        self.vl = nc.dram_tensor("vl", [TOK, D], BF16).ap()
        self.wbf = nc.dram_tensor("wbf", [D, 8 * D], BF16, addr_space="Shared").ap()
        self.kx = [nc.dram_tensor(f"kx{i}", [2 * D, TOK], BF16, addr_space="Shared").ap() for i in range(2)]
        self.vx = [nc.dram_tensor(f"vx{i}", [2 * TOK, D], BF16, addr_space="Shared").ap() for i in range(2)]
        self.shf = nc.dram_tensor("shf", [2, 16], I32, addr_space="Shared").ap()

        sb = lambda nm, shp, dt: ctx.enter_context(nc.sbuf_tensor(nm, shp, dt))
        self.bigA = sb("bigA", [128, NDJ * TOK], BF16)
        self.bigB = sb("bigB", [128, NTC * D], BF16)
        self.arena = sb("arena", [128, 32768], BF16)
        self.xt = self.arena[:, 0:16384].bitcast(F32)
        self.jnk = self.arena[:, 16384:20480]
        self.xnb = self.arena[:, 20480:24576]
        self.wsl = self.arena[:, 0:32768]
        self.ktile = self.arena[:, 0:8192]
        self.vtile = self.arena[:, 8192:16384]
        self.qtile = self.arena[:, 16384:20480]
        self.sqeff = self.arena[:, 20480:24576]
        self.spp = self.arena[:, 24576:28672]
        self.spt = self.arena[:, 28672:32768]
        self.xadd_in = self.arena[:, 0:16384].bitcast(F32)
        self.xadd_out = self.arena[:, 16384:32768].bitcast(F32)
        self.sohT = self.bigB[0:4, 0:1024]
        self.soh32 = self.bigB[0:32, 1024:2048]
        self.szT = self.bigB[0:32, 2048:3072]
        self.szm = self.bigB[0:32, 3072:4096]
        self.sbcat = self.bigB[0:32, 4096:8192]
        self.sd0g = self.bigB[:, 8192:9216].bitcast(F32)
        self.smhf = self.bigB[:, 9216:9472].bitcast(F32)
        self.sacat = self.bigB[:, 10240:11264]
        self.stats = sb("stats", [128, 32], F32)
        self.rstd = sb("rstd", [128, 8], F32)
        self.sbias = sb("sbias", [128, 128], F32)
        self.sbb = sb("sbb", [128, 2 * 512], BF16)
        self.seb = sb("seb", [128, 4], F32)
        self.sbw = sb("sbw", [128, 48], F32)
        self.sgw = sb("sgw", [128, NDJ * 4], BF16)
        self.sp48 = sb("sp48", [4, 32], BF16)
        self.sid = sb("sid", [128, 128], BF16)
        self.spar = sb("spar", [1, 1], I32)
        self.smag = sb("smag", [1, 16], I32)
        spoll2 = sb("spoll", [1, 2], I32)
        self.spoll = spoll2[:, 0:1]
        self.spollv = spoll2[:, 1:2]
        self.smh = sb("smh", [128, 16 * 128], BF16)
        self.scm = self.smh[:, 0:512].bitcast(F32)
        self.stgar = sb("stgar", [128, 4096], BF16)
        self.stg = self.stgar[:, 0:2048]
        self.stgf = self.stgar[:, 0:2048].bitcast(F32)
        self.sglog = sb("sglog", [128, NTC * 4], F32)
        self.sgmax = sb("sgmax", [128, NTC], F32)
        self.soh = sb("soh", [128, NTC * 4], BF16)
        self.sneg = sb("sneg", [128, 16], F32)
        self.sjnk = self.sgmax
        self.srl = self.sneg[:, 12:16]
        self.xsl = self.stgar[:, 2048:4096].bitcast(F32)
        self.PS = [ctx.enter_context(nc.psum_tensor("psA", [128, 2048], F32)),
                   ctx.enter_context(nc.psum_tensor("psB", [128, 2048], F32))]

        self.sem = {}
        for nm in ["lw0", "lw1", "lx0", "lx1", "ldp", "ldc", "ldb", "lk0", "lk1",
                   "lv0", "lv1", "cast", "cw0", "cw1", "cw2", "cw3", "sk0", "sk1",
                   "sx0", "sx1", "sex", "poll", "pollv", "pe", "dv", "ac", "gp",
                   "gpv", "dbg"]:
            self.sem[nm] = ctx.enter_context(nc.semaphore("s_" + nm))

        self.block = ctx.enter_context(nc.Block())

        self.emit_all()

        block = self.block
        steps = self.steps

        @block.tensor
        def _(t):
            r = {}
            for f in steps["t"]:
                f(t, r)

        @block.vector
        def _(v):
            r = {}
            for f in steps["v"]:
                f(v, r)

        @block.scalar
        def _(a):
            r = {}
            hs = [a.register(n) for n in ("prow", "fval", "pcnt", "cond", "mg")]
            for h, n in zip(hs, ("prow", "fval", "pcnt", "cond", "mg")):
                r[n] = h.__enter__()
            a.reg_mov(r["pcnt"], 0)
            for f in steps["a"]:
                f(a, r)
            for h in hs:
                h.__exit__(None, None, None)

        @block.gpsimd
        def _(g):
            r = {"gp": None}
            g_regs = g.register("gp")
            r["gp"] = g_regs.__enter__()
            for f in steps["g"]:
                f(g, r)
            g_regs.__exit__(None, None, None)

        @block.sync
        def _(s):
            r = {}
            hs = [s.register(n) for n in ("prow", "fval", "pcnt", "cond", "mg")]
            for h, n in zip(hs, ("prow", "fval", "pcnt", "cond", "mg")):
                r[n] = h.__enter__()
            s.reg_mov(r["pcnt"], 0)
            for f in steps["s"]:
                f(s, r)
            for h in hs:
                h.__exit__(None, None, None)

        self.ctx.close()
        return self.nc

    def emit_all(self):
        self.w_polled = [False] * 8
        self.w_cast_cnt = [0] * 8
        self.gp_mark = {}
        self.prelude()
        src = self.xc
        dsts = [self.xres[0], self.xres[1], self.xres[0], self.yc]
        if self.stage >= 1:
            d = dsts[0] if self.stage > 1 else self.yc
            self.sublayer_attn_a(src, d)
            src = d
        if self.stage >= 2:
            d = dsts[1] if self.stage > 2 else self.yc
            self.sublayer_ffn(src, d)
            src = d
        if self.stage >= 3:
            d = dsts[2] if self.stage > 3 else self.yc
            self.sublayer_attn_b(src, d)
            src = d
        if self.stage >= 4:
            self.sublayer_ffn(src, self.yc)
        self.waitall("s", ["sx0", "sx1", "sk0", "sk1", "sex"])

    def prelude(self):
        self.dma("s", self.spar[:], self.parI[:, :], "ldp")
        self.dma("s", self.smag[:], self.magI[:, :], "ldp")
        for dst, src in [(self.sbias[:], self.bcols[:, :]), (self.seb[:], self.ebI[:, :]),
                         (self.sbw[:], self.bwI[:, :]), (self.scm[:], self.cmI[:, :])]:
            self.dma("s", dst, src, "ldc")
        self.dma("g", self.sid[:], self.idI[:, :], "cast")
        self.dma("g", self.sgw[:, :].rearrange("p (a c) -> p a c", a=NDJ),
                 self.gwTI.rearrange("(a p) c -> p a c", p=128), "cast")
        self.dma("g", self.sp48[:, :], self.p48I[:, :], "cast")
        self.cast_pre = self.at("cast")
        self.bump("dv")
        sdvw = self.sem["dv"]
        self.st("v", lambda e, r, s=sdvw: e.memset(self.sjnk[:, :], 1.0).then_inc(s, 1))
        self.wait("a", "dv")
        self.bump("ac")
        sacw = self.sem["ac"]
        self.st("a", lambda e, r, s=sacw: e.activation(self.sjnk[:, 2:3], self.sjnk[:, 1:2], AF.Exp).then_inc(s, 1))
        self.wait("a", "ac")
        self.bump("ac")
        self.st("a", lambda e, r, s=sacw: e.activation(self.sjnk[:, 3:4], self.sjnk[:, 1:2], AF.Ln).then_inc(s, 1))
        self.bump("dv")
        sdv = self.sem["dv"]
        PS = self.PS
        self.st("v", lambda e, r: e.memset(PS[0][:, :], 0.0))
        self.st("v", lambda e, r, s=sdv, v=self.at("dv"): e.memset(PS[1][:, :], 0.0).then_inc(s, 1))
        self.psum_scrub = self.at("dv")
        self.waitall("g", ["ldp"])
        FLAG_GROUP = {0: 0, 1: 1, 2: 1, 3: 2, 4: 2, 5: 3, 6: 3, 7: 3}
        self.flag_group = FLAG_GROUP
        self.group_gp = {}
        GS = {0: "cw0", 1: "cw1", 2: "cw1", 3: "cw2", 4: "cw2", 5: "cw3", 6: "cw3", 7: "cw3"}
        self.gsem = GS
        for w in range(8):
            self.bump(GS[w], 16)
            scast = self.sem[GS[w]]

            def cast_w(e, r, w=w, s=scast):
                e.reg_load(r["gp"], self.spar[0:1, 0:1])
                e.reg_alu(r["gp"], r["gp"], 2048, ALU.mult)
                e.reg_alu(r["gp"], r["gp"], w * D, ALU.add)
                offw = e.snap(r["gp"])
                e.dma_start(out=self.wbf[:, bass.ds(offw, 2048)],
                            in_=self.wT[w][:, :]).then_inc(s, 16)
            self.st("g", cast_w)
            self.w_cast_cnt[w] = self.at(GS[w])
            if w in (0, 2, 4, 7):
                self.wait("g", GS[w])
                self.flag_write({0: 10, 2: 11, 4: 12, 7: 13}[w])

    def kv_exchange(self, attn_idx, flag_idx):
        self.waitall("s", ["sk0", "sk1"])
        sstk = self.sem["sex"]
        kxf, vxf = self.kx[attn_idx], self.vx[attn_idx]
        self.bump("sex", 16)

        def cpk(e, r, kxf=kxf, s=sstk):
            e.reg_load(r["prow"], self.spar[0:1, 0:1])
            e.reg_alu(r["prow"], r["prow"], D, ALU.mult)
            e.dma_start(out=kxf[bass.ds(e.snap(r["prow"]), D), :], in_=self.kl[:, :]).then_inc(s, 16)
        self.st("s", cpk)
        self.bump("sex", 16)

        def cpv(e, r, vxf=vxf, s=sstk):
            e.reg_load(r["prow"], self.spar[0:1, 0:1])
            e.reg_alu(r["prow"], r["prow"], TOK, ALU.mult)
            e.dma_start(out=vxf[bass.ds(e.snap(r["prow"]), TOK), :], in_=self.vl[:, :]).then_inc(s, 16)
        self.st("s", cpv)
        self.flag_write(flag_idx, extra_wait=("sex", self.at("sex")))
        return self.poll(flag_idx)

    def flag_write(self, idx, extra_wait=None):
        if extra_wait is not None:
            self.wait("g", extra_wait[0], extra_wait[1])
        self.bump("cast", 16)
        scast = self.sem["cast"]

        def f(e, r, idx=idx, s=scast):
            e.reg_load(r["gp"], self.spar[0:1, 0:1])
            e.dma_start(out=self.shf.rearrange("a b -> (a b)")[bass.ds(e.snap(r["gp"]), 1)]
                        if False else
                        self.shf[0:2, idx:idx + 1].rearrange("a b -> (a b) b")[bass.ds(e.snap(r["gp"]), 1), :]
                        if False else
                        self.shf[:, idx:idx + 1][bass.ds(e.snap(r["gp"]), 1), :],
                        in_=self.smag[0:1, idx:idx + 1]).then_inc(s, 16)
        self.st("g", f)

    def poll(self, idx, eng="s"):
        spoll = self.sem["poll" if eng == "s" else "pollv"]
        pbuf = self.spoll if eng == "s" else self.spollv
        assert eng in ("s", "a")

        def f(e, r, idx=idx, s=spoll, pbuf=pbuf):
            e.reg_load(r["prow"], self.spar[0:1, 0:1])
            e.reg_alu(r["prow"], r["prow"], 1, ALU.bitwise_xor)
            e.reg_load(r["mg"], self.smag[0:1, idx:idx + 1])
            e.reg_mov(r["cond"], 1)
            with e.While(r["cond"]):
                e.dma_start(out=pbuf[:],
                            in_=self.shf[:, idx:idx + 1][bass.ds(e.snap(r["prow"]), 1), :]).then_inc(s, 16)
                e.reg_alu(r["pcnt"], r["pcnt"], 16, ALU.add)
                e.wait_ge(s, e.snap(r["pcnt"]))
                e.reg_load(r["fval"], pbuf[0:1, 0:1])
                e.reg_alu(r["cond"], r["fval"], e.snap(r["mg"]), ALU.not_equal)
        self.st(eng, f)
        gsem = "gp" if eng == "s" else "gpv"
        self.bump(gsem)
        sgp = self.sem[gsem]
        self.st(eng, lambda e, r, s=sgp: e.sem_inc(s, 1))
        return self.at(gsem)

    def need_w(self, w):
        grp = self.flag_group[w]
        if grp not in self.group_gp:
            self.wait("a", ["cw0", "cw1", "cw2", "cw3"][grp])
            self.group_gp[grp] = self.poll({0: 10, 1: 11, 2: 12, 3: 13}[grp], eng="a")
        return self.group_gp[grp]

    def emit_ln(self, src):
        xn_done = {}
        ev_done = {}
        sdv, sac, spe = self.sem["dv"], self.sem["ac"], self.sem["pe"]
        for tc in range(NTC):
            b = tc % 2
            if tc >= 2:
                self.wait("s", "dv", xn_done[tc - 2])
            if tc == 0:
                self.waitall("s", ["sx0", "sx1"])
            self.dma("s", self.xt[:, b * D:(b + 1) * D], src[tc * 128:(tc + 1) * 128, :],
                     "lx0" if b == 0 else "lx1")
            self.wait("a", "lx0" if b == 0 else "lx1")
            self.wait("a", "pe")
            self.bump("ac")
            self.st("a", lambda e, r, b=b, c=tc, s=sac:
                    e.activation(self.xnb[:, :], self.xt[:, b * D:(b + 1) * D], AF.Copy,
                                 accum_out=self.stats[:, c * 4:c * 4 + 1]).then_inc(s, 1))
            self.bump("ac")
            self.st("a", lambda e, r, b=b, c=tc, s=sac:
                    e.activation(self.jnk[:, :], self.xt[:, b * D:(b + 1) * D], AF.Square,
                                 accum_out=self.stats[:, c * 4 + 1:c * 4 + 2]).then_inc(s, 1))
            self.wait("v", "ac")
            self.bump("dv")
            self.st("v", lambda e, r, c=tc, s=sdv:
                    e.tensor_scalar(self.stats[:, c * 4 + 2:c * 4 + 3], self.stats[:, c * 4:c * 4 + 1],
                                    -INVD, None, op0=ALU.mult).then_inc(s, 1))
            self.wait("v", "dv")
            self.bump("dv")
            self.st("v", lambda e, r, c=tc, s=sdv:
                    e.tensor_tensor(self.stats[:, c * 4 + 3:c * 4 + 4], self.stats[:, c * 4 + 2:c * 4 + 3],
                                    self.stats[:, c * 4 + 2:c * 4 + 3], op=ALU.mult).then_inc(s, 1))
            self.bump("dv")
            self.st("v", lambda e, r, c=tc, s=sdv:
                    e.tensor_scalar(self.rstd[:, c:c + 1], self.stats[:, c * 4 + 1:c * 4 + 2],
                                    INVD, EPS, op0=ALU.mult, op1=ALU.add).then_inc(s, 1))
            self.wait("v", "dv")
            self.bump("dv")
            self.st("v", lambda e, r, c=tc, s=sdv:
                    e.tensor_tensor(self.rstd[:, c:c + 1], self.rstd[:, c:c + 1],
                                    self.stats[:, c * 4 + 3:c * 4 + 4], op=ALU.subtract).then_inc(s, 1))
            self.wait("a", "dv")
            self.bump("ac")
            self.st("a", lambda e, r, c=tc, s=sac:
                    e.activation(self.rstd[:, c:c + 1], self.rstd[:, c:c + 1], AF.Ln).then_inc(s, 1))
            self.wait("v", "ac")
            self.bump("dv")
            self.st("v", lambda e, r, c=tc, s=sdv:
                    e.tensor_scalar(self.rstd[:, c:c + 1], self.rstd[:, c:c + 1], -0.5, None,
                                    op0=ALU.mult).then_inc(s, 1))
            self.wait("a", "dv")
            self.bump("ac")
            self.st("a", lambda e, r, c=tc, s=sac:
                    e.activation(self.rstd[:, c:c + 1], self.rstd[:, c:c + 1], AF.Exp).then_inc(s, 1))
            self.wait("v", "ac")
            self.wait("v", "pe")
            self.bump("dv")
            self.st("v", lambda e, r, b=b, c=tc, s=sdv:
                    e.tensor_scalar(self.xnb[:, :], self.xt[:, b * D:(b + 1) * D],
                                    self.stats[:, c * 4 + 2:c * 4 + 3], self.rstd[:, c:c + 1],
                                    op0=ALU.add, op1=ALU.mult).then_inc(s, 1))
            xn_done[tc] = self.at("dv")
            if tc == 0:
                self.wait("t", "cast")
            self.wait("t", "dv", xn_done[tc])
            if tc >= 2:
                self.wait("t", "dv", ev_done[tc - 2])
            ps = self.PS[b].bitcast(BF16)
            for dj in range(NDJ):
                if dj == NDJ - 1:
                    self.bump("pe")
                    self.st("t", lambda e, r, dj=dj, ps=ps, s=spe:
                            e.transpose(ps[:, dj * 128:(dj + 1) * 128],
                                        self.xnb[:, dj * 128:(dj + 1) * 128],
                                        self.sid[:, :]).then_inc(s, 1))
                else:
                    self.st("t", lambda e, r, dj=dj, ps=ps:
                            e.transpose(ps[:, dj * 128:(dj + 1) * 128],
                                        self.xnb[:, dj * 128:(dj + 1) * 128],
                                        self.sid[:, :]))
            self.wait("v", "pe")
            self.bump("dv")
            dst = self.bigA[:, :].rearrange("p (a c) -> p a c", a=NDJ)[:, :, tc * 128:(tc + 1) * 128]
            self.st("v", lambda e, r, ps=ps, dst=dst, s=sdv:
                    e.tensor_copy(dst, ps[:, 0:D].rearrange("p (a c) -> p a c", a=NDJ)).then_inc(s, 1))
            ev_done[tc] = self.at("dv")
        return ev_done[NTC - 1]

    def emit_fproj(self, w, bias_col0, scale, dest_kind, attn_idx=0, arena_ready=0):
        gpm = self.need_w(w)
        sdv, spe = self.sem["dv"], self.sem["pe"]
        mm_done, ev_done, st_done = {}, {}, {}
        for dog in range(16):
            b = dog % 2
            if dog >= 2:
                self.wait("a", "pe", mm_done[dog - 2])
            src = self.wbf[:, w * D + dog * 256:w * D + (dog + 1) * 256].rearrange("(a p) c -> p a c", p=128)
            dsl = self.wsl[:, b * NDJ * 512:(b + 1) * NDJ * 512] \
                .rearrange("p (a c) -> p a c", c=512)[:, :, 0:256]
            if dog == 0:
                self.wait("a", "pe")
                if arena_ready:
                    self.wait("a", "dv", arena_ready)
            self.dma("a", dsl, src, "lw0" if b == 0 else "lw1")
            self.wait("t", "lw0" if b == 0 else "lw1")
            self.wait("t", "gpv", gpm)
            if dog == 0:
                self.wait("t", "dv")
                self.wait("t", "ac")
            for doc in range(2):
                do = dog * 2 + doc
                ps = self.PS[do % 2][:, 0:1024]
                if do >= 2:
                    self.wait("t", "dv", ev_done[do - 2])
                for nh in range(2):
                    for di in range(NDJ):
                        lt = self.wsl[:, b * NDJ * 512 + di * 512 + doc * 128:
                                      b * NDJ * 512 + di * 512 + doc * 128 + 128]
                        rh = self.bigA[:, di * TOK + nh * 512: di * TOK + nh * 512 + 512]
                        last = (nh == 1 and di == NDJ - 1)
                        if last:
                            self.bump("pe")
                            self.st("t", lambda e, r, lt=lt, rh=rh, ps=ps, nh=nh, di=di, s=spe:
                                    e.matmul(ps[:, nh * 512:(nh + 1) * 512], lt, rh,
                                             start=(di == 0), stop=(di == NDJ - 1)).then_inc(s, 1))
                        else:
                            self.st("t", lambda e, r, lt=lt, rh=rh, ps=ps, nh=nh, di=di:
                                    e.matmul(ps[:, nh * 512:(nh + 1) * 512], lt, rh,
                                             start=(di == 0), stop=(di == NDJ - 1)))
                mm_done[dog] = self.at("pe")
                self.wait("v", "pe")
                if do >= 2:
                    self.wait("v", "sk0" if do % 2 == 0 else "sk1")
                else:
                    self.waitall("v", ["sk0", "sk1", "sx0", "sx1"])
                sdst = self.stg[:, (do % 2) * 1024:(do % 2 + 1) * 1024]
                bcol = self.sbias[:, bias_col0 + do:bias_col0 + do + 1]
                self.bump("dv")
                if scale is None:
                    self.st("v", lambda e, r, ps=ps, d=sdst, bc=bcol, s=sdv:
                            e.tensor_scalar(d, ps, bc, None, op0=ALU.add).then_inc(s, 1))
                else:
                    self.st("v", lambda e, r, ps=ps, d=sdst, bc=bcol, sc=scale, s=sdv:
                            e.tensor_scalar(d, ps, bc, sc, op0=ALU.add, op1=ALU.mult).then_inc(s, 1))
                ev_done[do] = self.at("dv")
                self.wait("s", "dv", ev_done[do])
                sks = "sk0" if do % 2 == 0 else "sk1"
                if dest_kind == "qx":
                    self.dma("s", self.qx[do * 128:(do + 1) * 128, :], sdst, sks)
                else:
                    self.dma("s", self.kl[do * 128:(do + 1) * 128, :], sdst, sks)
                st_done[do] = self.at(sks)
        return 0

    def emit_tproj(self, w, lhsT_arena, bias_bc_idx, out_kind, attn_idx=0,
                   with_lora=False, src_x=None, dst_x=None, arena_ready=None):
        gpm = self.need_w(w)
        sdv, spe = self.sem["dv"], self.sem["pe"]
        bb_ld = {}
        mm_done, ev_done, st_done = {}, {}, {}
        for dg in range(8):
            b = dg % 2
            if dg >= 2:
                self.wait("a", "pe", mm_done[dg - 2])
            src = self.wbf[:, w * D + dg * 512:w * D + (dg + 1) * 512].rearrange("(a p) c -> p a c", p=128)
            dsl = self.wsl[:, b * NDJ * 512:(b + 1) * NDJ * 512].rearrange("p (a c) -> p a c", c=512)
            if dg == 0:
                self.wait("a", "pe")
                if arena_ready:
                    self.wait("a", "dv", arena_ready)
            self.dma("a", dsl, src, "lw0" if b == 0 else "lw1")
            if dg == 0:
                self.wait("t", "dv")
                self.wait("t", "ac")
            if bias_bc_idx is not None:
                if dg >= 2:
                    self.wait("g", "dv", ev_done[(dg - 2) * NTC + NTC - 1])
                self.dma("g", self.sbb[:, b * 512:(b + 1) * 512],
                         self.bbcast[bias_bc_idx][:, dg * 512:(dg + 1) * 512], "ldb")
                bb_ld[dg] = self.at("ldb")
            self.wait("t", "lw0" if b == 0 else "lw1")
            self.wait("t", "gpv", gpm)
            if arena_ready:
                self.wait("t", "dv", arena_ready)
                arena_ready = None
            for tc in range(NTC):
                it = dg * NTC + tc
                ps = self.PS[it % 2][:, 0:512]
                if it >= 2:
                    self.wait("t", "dv", ev_done[it - 2])
                for di in range(NDJ):
                    lt = lhsT_arena[:, di * TOK + tc * 128: di * TOK + tc * 128 + 128]
                    rh = self.wsl[:, b * NDJ * 512 + di * 512:b * NDJ * 512 + (di + 1) * 512]
                    last = (di == NDJ - 1) and not with_lora
                    if last:
                        self.bump("pe")
                        self.st("t", lambda e, r, lt=lt, rh=rh, ps=ps, di=di, s=spe:
                                e.matmul(ps, lt, rh, start=(di == 0), stop=True).then_inc(s, 1))
                    else:
                        self.st("t", lambda e, r, lt=lt, rh=rh, ps=ps, di=di, wl=with_lora:
                                e.matmul(ps, lt, rh, start=(di == 0), stop=False if wl else (di == NDJ - 1)))
                if with_lora:
                    self.bump("pe")
                    self.st("t", lambda e, r, ps=ps, tc=tc, dg=dg, s=spe:
                            e.matmul(ps, self.szm[:, tc * 128:(tc + 1) * 128],
                                     self.sbcat[:, dg * 512:(dg + 1) * 512],
                                     start=False, stop=True).then_inc(s, 1))
                mm_done[dg] = self.at("pe")
                if out_kind == "vx":
                    self.wait("v", "pe")
                    self.wait("v", "ldb", bb_ld[dg])
                    if it >= 2:
                        self.wait("v", "sk0" if it % 2 == 0 else "sk1")
                    else:
                        self.waitall("v", ["sk0", "sk1", "sx0", "sx1"])
                    sdst = self.stg[:, (it % 2) * 1024:(it % 2) * 1024 + 512]
                    self.bump("dv")
                    self.st("v", lambda e, r, ps=ps, d=sdst, b=b, s=sdv:
                            e.scalar_tensor_tensor(d, ps, 1.0, self.sbb[:, b * 512:(b + 1) * 512],
                                                   op0=ALU.mult, op1=ALU.add).then_inc(s, 1))
                    ev_done[it] = self.at("dv")
                    self.wait("s", "dv", ev_done[it])
                    sks = "sk0" if it % 2 == 0 else "sk1"
                    self.dma("s", self.vl[tc * 128:(tc + 1) * 128, dg * 512:(dg + 1) * 512], sdst, sks)
                    st_done[it] = self.at(sks)
                else:
                    if it >= 2:
                        self.wait("s", "dv", ev_done[it - 2])
                    self.dma("s", self.xsl[:, (it % 2) * 512:(it % 2 + 1) * 512],
                             src_x[tc * 128:(tc + 1) * 128, dg * 512:(dg + 1) * 512],
                             "lx0" if it % 2 == 0 else "lx1")
                    self.wait("v", "pe")
                    self.wait("v", "lx0" if it % 2 == 0 else "lx1")
                    if it >= 2:
                        self.wait("v", "sx0" if it % 2 == 0 else "sx1")
                    else:
                        self.waitall("v", ["sk0", "sk1", "sx0", "sx1"])
                    sdst = self.stgf[:, (it % 2) * 512:(it % 2 + 1) * 512]
                    if bias_bc_idx is not None:
                        self.wait("v", "ldb", bb_ld[dg])
                        self.bump("dv")
                        self.st("v", lambda e, r, ps=ps, b=b, s=sdv:
                                e.scalar_tensor_tensor(ps, ps, 1.0, self.sbb[:, b * 512:(b + 1) * 512],
                                                       op0=ALU.mult, op1=ALU.add).then_inc(s, 1))
                        self.wait("v", "dv")
                    self.bump("dv")
                    self.st("v", lambda e, r, ps=ps, d=sdst, it=it, s=sdv:
                            e.tensor_tensor(d, ps, self.xsl[:, (it % 2) * 512:(it % 2 + 1) * 512],
                                            op=ALU.add).then_inc(s, 1))
                    ev_done[it] = self.at("dv")
                    self.wait("s", "dv", ev_done[it])
                    sxs = "sx0" if it % 2 == 0 else "sx1"
                    self.dma("s", dst_x[tc * 128:(tc + 1) * 128, dg * 512:(dg + 1) * 512], sdst, sxs)
                    st_done[it] = self.at(sxs)
        return 0

    def emit_attention(self, kind, kv_gp, only_first=0):
        sdv, sac, spe = self.sem["dv"], self.sem["ac"], self.sem["pe"]
        its = []
        if kind == "a":
            for h in range(32):
                for j in range(NTC):
                    its.append((h, j, (2 * j + 2) * 128))
        else:
            for h in range(16):
                for j in range(NTC):
                    its.append((h, j, 2048))
        n_it = len(its)
        aev_done = {}
        exp_done = {}
        ptev_done = {}
        rl_done = {}
        qk_done = {}
        tr_done = {}
        pv_done = {}
        kq_ld = {}
        v_ld = {}
        qe_done = {}

        def emit_head_loads(h):
            hb = h % 2
            if kind == "a":
                hp, hh = h // 2, h % 2
                lvs = "lv0" if hp % 2 == 0 else "lv1"
                lks = "lk0" if hb == 0 else "lk1"
                if hh == 0:
                    if hp >= 2:
                        self.wait("s", "pe", pv_done.get((hp - 2) * 16 + 15, 0))
                    for c in range(16):
                        self.dma("s", self.vtile[:, (hp % 2) * 4096 + c * 256:(hp % 2) * 4096 + (c + 1) * 256],
                                 self.vx[0][(c % 2) * TOK + (c // 2) * 128:(c % 2) * TOK + (c // 2) * 128 + 128,
                                            hp * 256:hp * 256 + 256], lvs)
                    v_ld[hp] = (lvs, self.at(lvs))
                if h >= 2:
                    self.wait("s", "pe", qk_done.get((h - 2) * 8 + 7, 0))
                for par in range(2):
                    self.dma("s", self.ktile[:, hb * 4096 + par * 1024:hb * 4096 + (par + 1) * 1024],
                             self.kx[0][par * D + h * 128:par * D + (h + 1) * 128, :], lks)
                self.dma("s", self.qtile[:, hb * 2048:hb * 2048 + 1024],
                         self.qx[h * 128:(h + 1) * 128, :], lks)
                kq_ld[h] = (lks, self.at(lks))
            else:
                lvs = "lv0" if hb == 0 else "lv1"
                lks = "lk0" if hb == 0 else "lk1"
                if h >= 2:
                    self.wait("s", "pe", qk_done.get((h - 2) * 8 + 7, 0))
                    self.wait("s", "pe", pv_done.get((h - 2) * 8 + 7, 0))
                for c in range(16):
                    self.dma("s", self.vtile[:, hb * 4096 + c * 256:hb * 4096 + (c + 1) * 256],
                             self.vx[1][(c // 8) * TOK + (c % 8) * 128:(c // 8) * TOK + (c % 8) * 128 + 128,
                                        h * 256:h * 256 + 256], lvs)
                for hdc in range(2):
                    for par in range(2):
                        self.dma("s", self.ktile[:, hb * 4096 + hdc * 2048 + par * 1024:
                                                 hb * 4096 + hdc * 2048 + (par + 1) * 1024],
                                 self.kx[1][par * D + h * 256 + hdc * 128:par * D + h * 256 + hdc * 128 + 128, :],
                                 lks)
                for hdc in range(2):
                    self.dma("s", self.qtile[:, hb * 2048 + hdc * 1024:hb * 2048 + (hdc + 1) * 1024],
                             self.qx[h * 256 + hdc * 128:h * 256 + hdc * 128 + 128, :], lks)
                kq_ld[h] = (lks, self.at(lks))
                v_ld[h] = (lvs, self.at(lvs))

        self.waitall("s", ["sk0", "sk1", "sex"])
        self.wait("t", "gp", kv_gp)
        self.wait("t", "dv")
        self.wait("t", "ac")

        def pe_emit_qk(it_idx):
            h, j, kl = its[it_idx]
            R = self.PS[it_idx % 2]
            if it_idx >= 2:
                self.wait("t", "dv", aev_done[it_idx - 2])
            else:
                self.wait("t", "dv", self.psum_scrub)
            self.wait("t", kq_ld[h][0], kq_ld[h][1])
            if kind == "a":
                nko = (j + 1) * 128
                hb = h % 2
                lt = self.qtile[:, hb * 2048 + j * 128: hb * 2048 + (j + 1) * 128]
                nmm = []
                for par in range(2):
                    base = par * 1024
                    o = 0
                    while o < nko:
                        n = min(512, nko - o)
                        nmm.append((base + o, n, par, o))
                        o += n
                for i2, (co, n, par, o) in enumerate(nmm):
                    last = i2 == len(nmm) - 1
                    rh = self.ktile[:, hb * 4096 + co: hb * 4096 + co + n]
                    if last:
                        self.bump("pe")
                        self.st("t", lambda e, r, R=R, co=co, n=n, lt=lt, rh=rh, s=spe:
                                e.matmul(R[:, co:co + n], lt, rh, start=True, stop=True).then_inc(s, 1))
                    else:
                        self.st("t", lambda e, r, R=R, co=co, n=n, lt=lt, rh=rh:
                                e.matmul(R[:, co:co + n], lt, rh, start=True, stop=True))
            else:
                hb = h % 2
                self.wait("t", "ac", qe_done[h])
                for co in range(0, 2048, 512):
                    for hdc in range(2):
                        lt = self.sqeff[:, hb * 2048 + hdc * 1024 + j * 128:
                                        hb * 2048 + hdc * 1024 + (j + 1) * 128]
                        rh = self.ktile[:, hb * 4096 + hdc * 2048 + co: hb * 4096 + hdc * 2048 + co + 512]
                        fin = co == 1536 and hdc == 1
                        if fin:
                            self.bump("pe")
                            self.st("t", lambda e, r, R=R, co=co, lt=lt, rh=rh, hdc=hdc, s=spe:
                                    e.matmul(R[:, co:co + 512], lt, rh, start=(hdc == 0),
                                             stop=(hdc == 1)).then_inc(s, 1))
                        else:
                            self.st("t", lambda e, r, R=R, co=co, lt=lt, rh=rh, hdc=hdc:
                                    e.matmul(R[:, co:co + 512], lt, rh, start=(hdc == 0), stop=(hdc == 1)))
            qk_done[it_idx] = self.at("pe")

        def ve_softmax(it_idx):
            h, j, kl = its[it_idx]
            R = self.PS[it_idx % 2]
            rn = it_idx % 4
            self.wait("v", "pe", qk_done[it_idx])
            if kind == "a":
                self.bump("dv")
                self.st("v", lambda e, r, R=R, j=j, s=sdv:
                        e.tensor_tensor(R[:, j * 128:(j + 1) * 128], R[:, j * 128:(j + 1) * 128],
                                        self.scm[:, 0:128], op=ALU.add).then_inc(s, 1))
                self.bump("dv")
                self.st("v", lambda e, r, R=R, j=j, s=sdv:
                        e.tensor_tensor(R[:, 1024 + j * 128:1024 + (j + 1) * 128],
                                        R[:, 1024 + j * 128:1024 + (j + 1) * 128],
                                        self.scm[:, 128:256], op=ALU.add).then_inc(s, 1))
                self.wait("v", "dv")
                nko = (j + 1) * 128
                self.bump("dv")
                self.st("v", lambda e, r, R=R, n=nko, rn=rn, s=sdv:
                        e.tensor_reduce(self.sneg[:, rn:rn + 1], R[:, 0:n], axis=AX.X,
                                        op=ALU.max, negate=True).then_inc(s, 1))
                self.bump("dv")
                self.st("v", lambda e, r, R=R, n=nko, rn=rn, s=sdv:
                        e.tensor_reduce(self.sneg[:, 4 + rn:5 + rn], R[:, 1024:1024 + n], axis=AX.X,
                                        op=ALU.max, negate=True).then_inc(s, 1))
                self.wait("v", "dv")
                self.bump("dv")
                self.st("v", lambda e, r, rn=rn, s=sdv:
                        e.tensor_tensor(self.sneg[:, rn:rn + 1], self.sneg[:, rn:rn + 1],
                                        self.sneg[:, 4 + rn:5 + rn], op=ALU.min).then_inc(s, 1))
            else:
                self.bump("dv")
                self.st("v", lambda e, r, R=R, rn=rn, s=sdv:
                        e.tensor_reduce(self.sneg[:, rn:rn + 1], R[:, :], axis=AX.X,
                                        op=ALU.max, negate=True).then_inc(s, 1))
            mx_done = self.at("dv")
            self.wait("a", "dv", mx_done)
            pb = self.spp[:, (it_idx % 2) * 2048:(it_idx % 2 + 1) * 2048]
            if kind == "a":
                nko = (j + 1) * 128
                self.bump("ac")
                self.st("a", lambda e, r, R=R, pb=pb, n=nko, rn=rn, s=sac:
                        e.activation(pb[:, 0:n], R[:, 0:n], AF.Exp, bias=self.sneg[:, rn:rn + 1],
                                     scale=1.0, accum_out=self.sneg[:, 8 + rn:9 + rn]).then_inc(s, 1))
                self.st("a", lambda e, r, R=R, pb=pb, n=nko, rn=rn:
                        e.activation(pb[:, 1024:1024 + n], R[:, 1024:1024 + n], AF.Exp,
                                     bias=self.sneg[:, rn:rn + 1], scale=1.0,
                                     accum_out=self.srl[:, rn:rn + 1]))
                self.bump("ac")
                self.st("a", lambda e, r, s=sac:
                        e.activation(self.sjnk[:, 0:1], self.sjnk[:, 1:2], AF.Copy).then_inc(s, 1))
                exp_done[it_idx] = self.at("ac")
                self.wait("v", "ac", exp_done[it_idx])
                self.bump("dv")
                self.st("v", lambda e, r, rn=rn, s=sdv:
                        e.tensor_tensor(self.srl[:, rn:rn + 1], self.srl[:, rn:rn + 1],
                                        self.sneg[:, 8 + rn:9 + rn], op=ALU.add).then_inc(s, 1))
                self.wait("v", "dv")
                self.bump("dv")
                self.st("v", lambda e, r, rn=rn, s=sdv:
                        e.reciprocal(self.srl[:, rn:rn + 1], self.srl[:, rn:rn + 1]).then_inc(s, 1))
            else:
                self.st("a", lambda e, r, R=R, pb=pb, rn=rn:
                        e.activation(pb[:, :], R[:, :], AF.Exp, bias=self.sneg[:, rn:rn + 1],
                                     scale=1.0, accum_out=self.srl[:, rn:rn + 1]))
                self.bump("ac")
                self.st("a", lambda e, r, s=sac:
                        e.activation(self.sjnk[:, 0:1], self.sjnk[:, 1:2], AF.Copy).then_inc(s, 1))
                exp_done[it_idx] = self.at("ac")
                self.wait("v", "ac", exp_done[it_idx])
                self.bump("dv")
                self.st("v", lambda e, r, rn=rn, s=sdv:
                        e.reciprocal(self.srl[:, rn:rn + 1], self.srl[:, rn:rn + 1]).then_inc(s, 1))
            rl_done[it_idx] = self.at("dv")

        def pe_emit_trans(it_idx):
            h, j, kl = its[it_idx]
            R = self.PS[it_idx % 2]
            Rb = R.bitcast(BF16)
            pb = self.spp[:, (it_idx % 2) * 2048:(it_idx % 2 + 1) * 2048]
            self.wait("t", "ac", exp_done[it_idx])
            nch = (2 * j + 2) if kind == "a" else 16
            for c in range(nch):
                if kind == "a":
                    scol = (c % 2) * 1024 + (c // 2) * 128
                else:
                    scol = c * 128
                last = c == nch - 1
                if last:
                    self.bump("pe")
                    self.st("t", lambda e, r, Rb=Rb, c=c, pb=pb, scol=scol, s=spe:
                            e.transpose(Rb[:, c * 128:(c + 1) * 128], pb[:, scol:scol + 128],
                                        self.sid[:, :]).then_inc(s, 1))
                else:
                    self.st("t", lambda e, r, Rb=Rb, c=c, pb=pb, scol=scol:
                            e.transpose(Rb[:, c * 128:(c + 1) * 128], pb[:, scol:scol + 128],
                                        self.sid[:, :]))
            tr_done[it_idx] = self.at("pe")
            nb = nch * 128
            ptb = self.spt[:, (it_idx % 2) * 2048:(it_idx % 2) * 2048 + nb]
            if it_idx % 2 == 0:
                self.wait("v", "pe", tr_done[it_idx])
                self.bump("dv")
                self.st("v", lambda e, r, Rb=Rb, d=ptb, nb=nb, s=sdv:
                        e.tensor_copy(d, Rb[:, 0:nb]).then_inc(s, 1))
                ptev_done[it_idx] = ("dv", self.at("dv"))
            else:
                self.wait("a", "pe", tr_done[it_idx])
                self.bump("ac")
                self.st("a", lambda e, r, Rb=Rb, d=ptb, nb=nb, s=sac:
                        e.activation(d, Rb[:, 0:nb], AF.Copy).then_inc(s, 1))
                ptev_done[it_idx] = ("ac", self.at("ac"))

        def pe_emit_pv(it_idx):
            h, j, kl = its[it_idx]
            R = self.PS[it_idx % 2]
            sk, sv_ = ptev_done[it_idx]
            self.wait("t", sk, sv_)
            nch = (2 * j + 2) if kind == "a" else 16
            hd = 128 if kind == "a" else 256
            if kind == "a":
                hp, hh = h // 2, h % 2
                self.wait("t", v_ld[hp][0], v_ld[hp][1])
                vb = (hp % 2) * 4096
                voff = hh * 128
            else:
                self.wait("t", v_ld[h][0], v_ld[h][1])
                vb = (h % 2) * 4096
                voff = 0
            pvp = R[:, 1536:1536 + hd]
            for c in range(nch):
                lt = self.spt[:, (it_idx % 2) * 2048 + c * 128:(it_idx % 2) * 2048 + (c + 1) * 128]
                rh = self.vtile[:, vb + c * 256 + voff: vb + c * 256 + voff + hd]
                last = c == nch - 1
                if last:
                    self.bump("pe")
                    self.st("t", lambda e, r, pvp=pvp, lt=lt, rh=rh, c=c, nch=nch, s=spe:
                            e.matmul(pvp, lt, rh, start=(c == 0), stop=(c == nch - 1)).then_inc(s, 1))
                else:
                    self.st("t", lambda e, r, pvp=pvp, lt=lt, rh=rh, c=c, nch=nch:
                            e.matmul(pvp, lt, rh, start=(c == 0), stop=(c == nch - 1)))
            pv_done[it_idx] = self.at("pe")
            self.wait("v", "pe", pv_done[it_idx])
            self.wait("v", "dv", rl_done[it_idx])
            rn = it_idx % 4
            dcol = j * D + h * hd
            self.bump("dv")
            self.st("v", lambda e, r, pvp=pvp, dcol=dcol, hd=hd, rn=rn, s=sdv:
                    e.tensor_scalar(self.bigB[:, dcol:dcol + hd], pvp, self.srl[:, rn:rn + 1],
                                    None, op0=ALU.mult).then_inc(s, 1))
            aev_done[it_idx] = self.at("dv")

        def emit_qeff(h):
            it_idx = h * 8
            hb = h % 2
            R = self.PS[it_idx % 2]
            if it_idx >= 2:
                self.wait("t", "dv", aev_done[it_idx - 2])
            else:
                self.wait("t", "dv", self.psum_scrub)
            self.wait("t", kq_ld[h][0], kq_ld[h][1])
            self.wait("t", self.mh_done[0], self.mh_done[1])
            for hdc in range(2):
                for nh in range(2):
                    last = hdc == 1 and nh == 1
                    lt = self.smh[:, h * 128:(h + 1) * 128]
                    rh = self.qtile[:, hb * 2048 + hdc * 1024 + nh * 512:
                                    hb * 2048 + hdc * 1024 + (nh + 1) * 512]
                    if last:
                        self.bump("pe")
                        self.st("t", lambda e, r, R=R, hdc=hdc, nh=nh, lt=lt, rh=rh, s=spe:
                                e.matmul(R[:, hdc * 1024 + nh * 512:hdc * 1024 + (nh + 1) * 512],
                                         lt, rh, start=True, stop=True).then_inc(s, 1))
                    else:
                        self.st("t", lambda e, r, R=R, hdc=hdc, nh=nh, lt=lt, rh=rh:
                                e.matmul(R[:, hdc * 1024 + nh * 512:hdc * 1024 + (nh + 1) * 512],
                                         lt, rh, start=True, stop=True))
            qe_pe = self.at("pe")
            self.wait("a", "pe", qe_pe)
            self.bump("ac")
            self.st("a", lambda e, r, R=R, hb=hb, s=sac:
                    e.activation(self.sqeff[:, hb * 2048:(hb + 1) * 2048], R[:, :], AF.Copy).then_inc(s, 1))
            qe_done[h] = self.at("ac")

        heads = 32 if kind == "a" else 16
        if only_first:
            n = abs(only_first)
            loads_only_tail = only_first < 0
            nh_full = (n + 7) // 8
            for h in range(nh_full):
                emit_head_loads(h)
                if kind == "b":
                    emit_qeff(h)
                for j in range(min(8, n - h * 8)):
                    it_idx = h * 8 + j
                    pe_emit_qk(it_idx)
                    ve_softmax(it_idx)
                    if it_idx > 0:
                        pe_emit_trans(it_idx - 1)
                        pe_emit_pv(it_idx - 1)
            pe_emit_trans(n - 1)
            pe_emit_pv(n - 1)
            if loads_only_tail:
                emit_head_loads(nh_full)
            return aev_done[n - 1]
        for h in range(heads):
            emit_head_loads(h)
            if kind == "b":
                emit_qeff(h)
            for j in range(NTC):
                it_idx = h * 8 + j
                pe_emit_qk(it_idx)
                ve_softmax(it_idx)
                if it_idx > 0:
                    pe_emit_trans(it_idx - 1)
                    pe_emit_pv(it_idx - 1)
        pe_emit_trans(n_it - 1)
        pe_emit_pv(n_it - 1)
        return aev_done[n_it - 1]

    def dump_sbuf(self, ap_bf16):
        self.wait("g", "dv")
        self.wait("g", "ac")
        self.wait("g", "pe")
        ydst = self.yc.rearrange("a b -> (a b)").rearrange("(p c) -> p c", p=128)
        self.dma("g", ydst, ap_bf16, "dbg")
        self.wait("s", "dbg")

    def dump_dram(self, src_ap, rows):
        self.wait("g", "dv")
        self.waitall("g", ["sk0", "sk1", "sx0", "sx1", "sex"])
        ydst = self.yc.rearrange("a b -> (a b)")[0:rows * 1024].rearrange("(a c) -> a c", c=1024)
        self.dma("g", ydst, src_ap, "dbg")
        self.wait("s", "dbg")

    def sublayer_attn_a(self, src, dst):
        if DEBUG_STOP == "prelude":
            self.dump_sbuf(self.bigA[:, :])
            return
        ln_ev = self.emit_ln(src)
        if DEBUG_STOP == "ln":
            self.wait("g", "dv", ln_ev)
            self.dump_sbuf(self.bigA[:, :])
            return
        self.emit_fproj(W_QA, 0, C_A, "qx", arena_ready=ln_ev)
        if DEBUG_STOP == "qproj":
            self.dump_dram(self.qx.rearrange("a b -> (a b)").rearrange("(a c) -> a c", c=1024), 4096)
            return
        self.emit_fproj(W_KA, 32, None, "kx", attn_idx=0)
        self.emit_tproj(W_VA, self.bigA, 0, "vx", attn_idx=0)
        if DEBUG_STOP == "kv":
            self.dump_dram(self.vl.rearrange("a b -> (a b)").rearrange("(a c) -> a c", c=1024), 4096)
            return
        kv_gp = self.kv_exchange(0, FLAG_KVA)
        if isinstance(DEBUG_STOP, str) and DEBUG_STOP.startswith("attnN"):
            n = int(DEBUG_STOP[5:])
            a_done = self.emit_attention("a", kv_gp, only_first=n)
            self.wait("g", "dv", a_done)
            self.waitall("g", ["lk0", "lk1", "lv0", "lv1"])
            self.dump_sbuf(self.bigB[:, :])
            return
        if DEBUG_STOP == "exch":
            self.wait("g", "gp", kv_gp)
            self.dump_dram(self.vx[0].rearrange("a b -> (a b)")[0:TOK * D].rearrange("(a c) -> a c", c=1024), 4096)
            return
        a_done = self.emit_attention("a", kv_gp)
        if DEBUG_STOP == "attn":
            self.wait("g", "dv", a_done)
            self.dump_sbuf(self.bigB[:, :])
            return
        sdv, spe = self.sem["dv"], self.sem["pe"]
        ev_done = {}
        self.wait("t", "dv", a_done)
        for tc in range(NTC):
            ps = self.PS[tc % 2].bitcast(BF16)
            if tc >= 2:
                self.wait("t", "dv", ev_done[tc - 2])
            for dj in range(NDJ):
                last = dj == NDJ - 1
                srcb = self.bigB[:, tc * D + dj * 128: tc * D + (dj + 1) * 128]
                if last:
                    self.bump("pe")
                    self.st("t", lambda e, r, ps=ps, dj=dj, srcb=srcb, s=spe:
                            e.transpose(ps[:, dj * 128:(dj + 1) * 128], srcb, self.sid[:, :]).then_inc(s, 1))
                else:
                    self.st("t", lambda e, r, ps=ps, dj=dj, srcb=srcb:
                            e.transpose(ps[:, dj * 128:(dj + 1) * 128], srcb, self.sid[:, :]))
            self.wait("v", "pe")
            self.bump("dv")
            dstA = self.bigA[:, :].rearrange("p (a c) -> p a c", a=NDJ)[:, :, tc * 128:(tc + 1) * 128]
            self.st("v", lambda e, r, ps=ps, dstA=dstA, s=sdv:
                    e.tensor_copy(dstA, ps[:, 0:D].rearrange("p (a c) -> p a c", a=NDJ)).then_inc(s, 1))
            ev_done[tc] = self.at("dv")
        self.emit_tproj(W_OA, self.bigA, 1, "addx", src_x=src, dst_x=dst,
                        arena_ready=ev_done[NTC - 1])

    def sublayer_ffn(self, src, dst):
        ln_ev = self.emit_ln(src)
        sdv, sac, spe = self.sem["dv"], self.sem["ac"], self.sem["pe"]
        self.wait("t", "cast", self.cast_pre)
        self.wait("t", "dv", ln_ev)
        self.wait("g", "dv")
        self.dma("g", self.sbcat[:, :], self.bcatI[:, :], "ldb")
        self.dma("g", self.sacat[:, :].rearrange("p (a c) -> p a c", a=NDJ),
                 self.acatI.rearrange("(a p) c -> p a c", p=128), "ldb")
        self.wait("t", "ldb")
        glog_done = {}
        for tc in range(NTC):
            ps = self.PS[0][:, tc * 8:tc * 8 + 4]
            for di in range(NDJ):
                lt = self.bigA[:, di * TOK + tc * 128: di * TOK + tc * 128 + 128]
                rh = self.sgw[:, di * 4:(di + 1) * 4]
                last = di == NDJ - 1
                if last:
                    self.bump("pe")
                    self.st("t", lambda e, r, ps=ps, lt=lt, rh=rh, di=di, s=spe:
                            e.matmul(ps, lt, rh, start=(di == 0), stop=(di == NDJ - 1)).then_inc(s, 1))
                else:
                    self.st("t", lambda e, r, ps=ps, lt=lt, rh=rh, di=di:
                            e.matmul(ps, lt, rh, start=(di == 0), stop=(di == NDJ - 1)))
            glog_done[tc] = self.at("pe")
        zps = self.PS[1]
        for nh in range(2):
            for di in range(NDJ):
                lt = self.sacat[:, di * 32:(di + 1) * 32]
                rh = self.bigA[:, di * TOK + nh * 512: di * TOK + (nh + 1) * 512]
                last = di == NDJ - 1
                if last:
                    self.bump("pe")
                    self.st("t", lambda e, r, zps=zps, lt=lt, rh=rh, nh=nh, di=di, s=spe:
                            e.matmul(zps[0:32, nh * 512:(nh + 1) * 512], lt, rh,
                                     start=(di == 0), stop=(di == NDJ - 1)).then_inc(s, 1))
                else:
                    self.st("t", lambda e, r, zps=zps, lt=lt, rh=rh, nh=nh, di=di:
                            e.matmul(zps[0:32, nh * 512:(nh + 1) * 512], lt, rh,
                                     start=(di == 0), stop=(di == NDJ - 1)))
        z_done = self.at("pe")
        oh_done = {}
        for tc in range(NTC):
            self.wait("v", "pe", glog_done[NTC - 1])
            self.wait("v", "ldc")
            ps = self.PS[0][:, tc * 8:tc * 8 + 4]
            self.bump("dv")
            self.st("v", lambda e, r, ps=ps, tc=tc, s=sdv:
                    e.tensor_tensor(self.sglog[:, tc * 4:(tc + 1) * 4], ps, self.seb[:, :],
                                    op=ALU.add).then_inc(s, 1))
            self.wait("v", "dv")
            self.bump("dv")
            self.st("v", lambda e, r, tc=tc, s=sdv:
                    e.tensor_reduce(self.sgmax[:, tc:tc + 1], self.sglog[:, tc * 4:(tc + 1) * 4],
                                    axis=AX.X, op=ALU.max).then_inc(s, 1))
            self.wait("v", "dv")
            self.bump("dv")
            self.st("v", lambda e, r, tc=tc, s=sdv:
                    e.tensor_scalar(self.soh[:, tc * 4:(tc + 1) * 4], self.sglog[:, tc * 4:(tc + 1) * 4],
                                    self.sgmax[:, tc:tc + 1], 1.0, op0=ALU.is_equal,
                                    op1=ALU.mult).then_inc(s, 1))
            oh_done[tc] = self.at("dv")
        ohT_ps = self.PS[0].bitcast(BF16)
        for tc in range(NTC):
            self.wait("t", "dv", oh_done[tc])
            last = tc == NTC - 1
            if last:
                self.bump("pe")
                self.st("t", lambda e, r, tc=tc, s=spe:
                        e.transpose(ohT_ps[0:4, 2048 + tc * 128:2048 + (tc + 1) * 128],
                                    self.soh[:, tc * 4:(tc + 1) * 4], self.sid[:, :]).then_inc(s, 1))
            else:
                self.st("t", lambda e, r, tc=tc:
                        e.transpose(ohT_ps[0:4, 2048 + tc * 128:2048 + (tc + 1) * 128],
                                    self.soh[:, tc * 4:(tc + 1) * 4], self.sid[:, :]))
        ohT_done = self.at("pe")
        self.wait("v", "pe", ohT_done)
        self.bump("dv")
        self.st("v", lambda e, r, s=sdv:
                e.tensor_copy(self.sohT[:, :], ohT_ps[0:4, 2048:2048 + 1024]).then_inc(s, 1))
        self.wait("v", "pe", z_done)
        self.bump("dv")
        self.st("v", lambda e, r, zps=zps, s=sdv:
                e.tensor_copy(self.szT[:, :], zps[0:32, 0:1024]).then_inc(s, 1))
        zev = self.at("dv")
        self.wait("t", "dv", zev)
        for nh in range(2):
            last = nh == 1
            if last:
                self.bump("pe")
                self.st("t", lambda e, r, zps=zps, nh=nh, s=spe:
                        e.matmul(zps[0:32, 1024 + nh * 512:1024 + (nh + 1) * 512],
                                 self.sp48[:, :], self.sohT[0:4, nh * 512:(nh + 1) * 512],
                                 start=True, stop=True).then_inc(s, 1))
            else:
                self.st("t", lambda e, r, zps=zps, nh=nh:
                        e.matmul(zps[0:32, 1024 + nh * 512:1024 + (nh + 1) * 512],
                                 self.sp48[:, :], self.sohT[0:4, nh * 512:(nh + 1) * 512],
                                 start=True, stop=True))
        oh32_done = self.at("pe")
        self.wait("v", "pe", oh32_done)
        self.bump("dv")
        self.st("v", lambda e, r, zps=zps, s=sdv:
                e.tensor_copy(self.soh32[:, :], zps[0:32, 1024:2048]).then_inc(s, 1))
        self.wait("v", "dv")
        self.bump("dv")
        self.st("v", lambda e, r, s=sdv:
                e.tensor_tensor(self.szm[:, :], self.szT[:, :], self.soh32[:, :],
                                op=ALU.mult).then_inc(s, 1))
        zm_done = self.at("dv")
        self.emit_tproj(W_UP, self.bigA, None, "addx", with_lora=True,
                        src_x=src, dst_x=dst, arena_ready=zm_done)

    def sublayer_attn_b(self, src, dst):
        sdv, sac = self.sem["dv"], self.sem["ac"]
        self.wait("s", "dv")
        self.dma("s", self.sd0g[:, :].rearrange("p (a c) -> p a c", a=4),
                 self.d0gI.rearrange("a p c -> p a c"), "ldc")
        self.wait("v", "ldc")
        mh_prev = 0
        for h in range(16):
            for c in range(3):
                self.bump("dv")
                in1 = self.sd0g[:, 0:128] if c == 0 else self.smhf[:, :]
                self.st("v", lambda e, r, h=h, c=c, in1=in1, s=sdv:
                        e.scalar_tensor_tensor(self.smhf[:, :], self.sd0g[:, (c + 1) * 128:(c + 2) * 128],
                                               self.sbw[:, h * 3 + c:h * 3 + c + 1], in1,
                                               op0=ALU.mult, op1=ALU.add).then_inc(s, 1))
                self.wait("v", "dv")
            self.wait("a", "dv")
            self.bump("ac")
            self.st("a", lambda e, r, h=h, s=sac:
                    e.activation(self.smh[:, h * 128:(h + 1) * 128], self.smhf[:, :], AF.Copy).then_inc(s, 1))
            self.wait("v", "ac")
        self.mh_done = ("ac", self.at("ac"))

        if DEBUG_STOP == "x2":
            self.dump_dram(src.rearrange("a b -> (a b)").rearrange("(a c) -> a c", c=1024), 4096)
            return
        ln_ev = self.emit_ln(src)
        if DEBUG_STOP == "lnb":
            self.wait("g", "dv", ln_ev)
            self.dump_sbuf(self.bigA[:, :])
            return
        self.emit_fproj(W_QB, 64, None, "qx", arena_ready=ln_ev)
        self.emit_fproj(W_KB, 96, None, "kx", attn_idx=1)
        self.emit_tproj(W_VB, self.bigA, 2, "vx", attn_idx=1)
        if DEBUG_STOP == "kvb":
            self.dump_dram(self.vl.rearrange("a b -> (a b)").rearrange("(a c) -> a c", c=1024), 4096)
            return
        kv_gp = self.kv_exchange(1, FLAG_KVB)
        a_done = self.emit_attention("b", kv_gp)
        if DEBUG_STOP == "attnb0":
            self.emit_attention("b", kv_gp, only_first=1)
            self.waitall("g", ["lk0", "lk1", "lv0", "lv1"])
            self.wait("g", "dv")
            self.wait("g", "ac")
            self.dump_sbuf(self.arena[:, :])
            return
        if DEBUG_STOP == "attnb":
            self.wait("g", "dv", a_done)
            self.waitall("g", ["lk0", "lk1", "lv0", "lv1"])
            self.dump_sbuf(self.bigB[:, :])
            return
        spe = self.sem["pe"]
        st_done = {}
        self.wait("s", "dv", a_done)
        for tc in range(NTC):
            b = tc % 2
            if tc >= 2:
                self.wait("s", "sx0" if b == 0 else "sx1")
            self.dma("s", self.xadd_in[:, 0:D] if b == 0 else self.xadd_in[:, D:2 * D],
                     src[tc * 128:(tc + 1) * 128, :], "lx0" if b == 0 else "lx1")
            self.wait("v", "lx0" if b == 0 else "lx1")
            self.bump("dv")
            xin = self.xadd_in[:, 0:D] if b == 0 else self.xadd_in[:, D:2 * D]
            xout = self.xadd_out[:, 0:D] if b == 0 else self.xadd_out[:, D:2 * D]
            self.st("v", lambda e, r, xin=xin, xout=xout, tc=tc, s=sdv:
                    e.tensor_tensor(xout, xin, self.bigB[:, tc * D:(tc + 1) * D], op=ALU.add).then_inc(s, 1))
            self.wait("s", "dv")
            sxs = "sx0" if b == 0 else "sx1"
            self.dma("s", dst[tc * 128:(tc + 1) * 128, :], xout, sxs)
            st_done[tc] = self.at(sxs)


def build_nc(stage=STAGE):
    nc = bass.Bass("TRN2", target_bir_lowering=False, debug=False)
    nc.detect_race_conditions = False
    kb = KB(nc, stage)
    return kb.build()


def _tri_mask():
    m = np.zeros((128, 128), dtype=np.float32)
    for q in range(128):
        m[q, q + 1:] = NEG
    return m


def _prep_inputs(inputs):
    x = np.asarray(inputs["L0_attn_a_x"])
    outs = []
    wts = [inputs["L0_attn_a_Wq"], inputs["L0_attn_a_Wk"], inputs["L0_attn_a_Wv"],
           inputs["L0_attn_a_Wo"], inputs["L0_ffn_W_up"], inputs["L0_attn_b_Wq"],
           inputs["L0_attn_b_Wk"], inputs["L0_attn_b_Wv"]]
    wTf = [np.asarray(w).T.astype(np.float32) for w in wts]
    bcols = np.zeros((128, 128), dtype=np.float32)
    for i, nm in enumerate(["L0_attn_a_bq", "L0_attn_a_bk", "L0_attn_b_bq", "L0_attn_b_bk"]):
        bcols[:, i * 32:(i + 1) * 32] = np.asarray(inputs[nm]).reshape(32, 128).T
    bbcast = np.stack([
        np.broadcast_to(np.asarray(inputs["L0_attn_a_bv"]), (128, D)),
        np.broadcast_to(np.asarray(inputs["L0_attn_a_bo"]), (128, D)),
        np.broadcast_to(np.asarray(inputs["L0_attn_b_bv"]), (128, D)),
    ]).astype(np.float32)
    eb = np.broadcast_to(np.asarray(inputs["L0_ffn_expert_bias"]), (128, 4)).astype(np.float32)
    bw = np.broadcast_to(np.asarray(inputs["L0_attn_b_bias_weight"]).reshape(1, 48), (128, 48)).astype(np.float32)
    gwT = np.ascontiguousarray(np.asarray(inputs["L0_ffn_gate_w"]).T.astype(np.float32))
    acatT = np.ascontiguousarray(np.asarray(inputs["L0_ffn_lora_A"]).reshape(32, D).T.astype(np.float32))
    bcatP = np.ascontiguousarray(np.asarray(inputs["L0_ffn_lora_B"]).transpose(0, 2, 1).reshape(32, D).astype(np.float32))
    p48 = np.kron(np.eye(4, dtype=np.float32), np.ones((1, 8), dtype=np.float32))
    ident = np.eye(128, dtype=np.float32)
    d0g = _d0g_consts()
    tri = _tri_mask()
    full = np.full((128, 128), NEG, dtype=np.float32)
    zero = np.zeros((128, 128), dtype=np.float32)
    cm = {0: np.concatenate([tri, full], axis=1),
          1: np.concatenate([zero, tri], axis=1)}
    in_maps = []
    for c in range(8):
        b, p = c // 2, c % 2
        xcb = np.ascontiguousarray(x[b].reshape(16, 128, D)[p::2].reshape(TOK, D).astype(np.float32))
        m = {"xc": xcb, "bcols": bcols, "bbcast": bbcast, "ebI": eb, "bwI": bw,
             "gwTI": gwT, "acatI": acatT, "bcatI": bcatP, "p48I": p48,
             "idI": ident, "cmI": cm[p], "d0gI": d0g,
             "parI": np.array([[p]], dtype=np.int32),
             "magI": MAGICS.reshape(1, 16)}
        for i in range(8):
            m[f"w{i}"] = np.ascontiguousarray(wTf[i][:, p * 2048:(p + 1) * 2048])
        in_maps.append(m)
    return in_maps


def kernel(**inputs):
    in_maps = _prep_inputs(inputs)
    nc = build_nc(STAGE)
    res = run_bass_kernel_spmd(nc, in_maps, core_ids=list(range(8)))
    x = np.asarray(inputs["L0_attn_a_x"])
    out = np.empty((4, 2048, D), dtype=np.float32)
    for c in range(8):
        b, p = c // 2, c % 2
        o = out[b].reshape(16, 128, D)
        o[p::2] = np.asarray(res.results[c]["yc"]).reshape(NTC, 128, D)
    return out
